# version 05
# baseline (speedup 1.0000x reference)
import numpy as np
import ml_dtypes
from contextlib import ExitStack

import concourse.bass as bass
import concourse.mybir as mybir
from concourse.bass_utils import run_bass_kernel_spmd

F32 = mybir.dt.float32
BF16 = mybir.dt.bfloat16
AF = mybir.ActivationFunctionType
ALU = mybir.AluOpType
AX = mybir.AxisListType

N_CORES = 8


class SemT:
    def __init__(self, handle, unit):
        self.h = handle
        self.unit = unit

    def exp(self, k):
        return k * self.unit


def _inc(inst, sem):
    inst.then_inc(sem.h, sem.unit)
    return inst


def build_program(T=2048, EMB=4096, FF=11008, detect_races=False, phases=4, p0mask=15, p1mask=15):
    assert EMB % 512 == 0 and T % 512 == 0 and FF % 256 == 0
    EC = EMB // 128
    H = EMB // 128
    J = H // 4
    NF = FF // 128
    TCH = 512
    NTCH = T // TCH
    NSUB = TCH // 128
    NQ1 = T // 512
    NT0 = T // 128
    NTG = T // 128
    EP = 512
    NEP = EMB // EP
    NGT = NSUB * 8
    VC = 4 * (J + 1)
    SCALE = 1.0 / float(np.sqrt(128.0))
    OCQ = 3 * J
    PVC = 3 + 3 * J

    nc = bass.Bass("TRN2", target_bir_lowering=False,
                   detect_race_conditions=detect_races)

    x_in = nc.dram_tensor("x", [T, EMB], F32, kind="ExternalInput").ap()
    wq = nc.dram_tensor("wq", [EMB, EMB], F32, kind="ExternalInput").ap()
    wk = nc.dram_tensor("wk", [EMB, 128 * J], F32, kind="ExternalInput").ap()
    wv = nc.dram_tensor("wv", [EMB, 128 * J], F32, kind="ExternalInput").ap()
    bq = nc.dram_tensor("bq", [128, H], F32, kind="ExternalInput").ap()
    bk = nc.dram_tensor("bk", [128, J], F32, kind="ExternalInput").ap()
    bv = nc.dram_tensor("bv", [128, J], F32, kind="ExternalInput").ap()
    wg = nc.dram_tensor("wg", [FF, EMB], F32, kind="ExternalInput").ap()
    wu = nc.dram_tensor("wu", [FF, EMB], F32, kind="ExternalInput").ap()
    wd = nc.dram_tensor("wd", [EMB, FF], F32, kind="ExternalInput").ap()
    out = nc.dram_tensor("out", [T, EMB], F32, kind="ExternalOutput").ap()

    wgb = nc.dram_tensor("wgb", [FF, EMB], BF16, kind="Internal").ap()
    wub = nc.dram_tensor("wub", [FF, EMB], BF16, kind="Internal").ap()
    wdb = nc.dram_tensor("wdb", [EMB, FF], BF16, kind="Internal").ap()
    wqgb = nc.dram_tensor("wqgb", [J * EMB * 128], BF16, kind="Internal").ap()
    wkb = nc.dram_tensor("wkb", [J * EMB * 128], BF16, kind="Internal").ap()
    wvb = nc.dram_tensor("wvb", [J * EMB * 128], BF16, kind="Internal").ap()
    pts = nc.dram_tensor("pts", [NF * T * 128], BF16, kind="Internal").ap()
    xnb = nc.dram_tensor("xnb", [128, EC * T], BF16, kind="Internal").ap()
    attn_s = nc.dram_tensor("attn_s", [T, EMB], F32, kind="Internal").ap()

    with ExitStack() as root:
        def SEM(name, unit=1):
            return SemT(root.enter_context(nc.semaphore(name)), unit)

        s_x = SEM("s_x", 16)
        s_ln = SEM("s_ln")
        s_sc0 = SEM("s_sc0")
        s_xt = SEM("s_xt")
        s_xe = SEM("s_xe")
        s_cg = SEM("s_cg", 16)
        s_cu = SEM("s_cu", 16)
        s_cd = SEM("s_cd", 16)
        s_pq = SEM("s_pq", 16)
        s_pk = SEM("s_pk", 16)
        s_sk = SEM("s_sk", 16)
        s_sv = SEM("s_sv", 16)
        s_pv = SEM("s_pv")
        s_pw = SEM("s_pw", 16)
        s_ini = SEM("s_ini")
        s_tr = SEM("s_tr", 16)
        s_mg = SEM("s_mg")
        s_mu = SEM("s_mu")
        s_mf = SEM("s_mf")
        s_si = SEM("s_si")
        s_ml = SEM("s_ml")
        s_pt = SEM("s_pt", 16)
        s_xb = SEM("s_xb", 16)
        s_x2 = SEM("s_x2", 16)
        s_wc = SEM("s_wc", 16)
        s_m2 = SEM("s_m2")
        s_e2 = SEM("s_e2")
        s_t2 = SEM("s_t2")
        s_te = SEM("s_te")
        s_bl = SEM("s_bl", 16)
        s_m1 = SEM("s_m1")
        s_rm = SEM("s_rm")
        s_nt = SEM("s_nt")
        s_ne = SEM("s_ne")
        s_in = SEM("s_in", 16)
        s_mT = SEM("s_mT")
        s_ex = SEM("s_ex")
        s_m3 = SEM("s_m3")
        s_rc = SEM("s_rc")
        s_no = SEM("s_no")
        s_as = SEM("s_as", 16)
        s_wt = SEM("s_wt", 16)
        s_p4 = SEM("s_p4", 16)
        s_xa = SEM("s_xa", 16)
        s_my = SEM("s_my")
        s_cb = SEM("s_cb")
        s_ou = SEM("s_ou", 16)

        ident = root.enter_context(nc.sbuf_tensor("ident", [128, 128], BF16))
        bqg = root.enter_context(nc.sbuf_tensor("bqg", [128, J], F32))
        bkc = root.enter_context(nc.sbuf_tensor("bkc", [128, J], F32))
        bvc = root.enter_context(nc.sbuf_tensor("bvc", [128, J], F32))
        bqt = root.enter_context(nc.sbuf_tensor("bqt", [128, H], F32))
        bq2 = root.enter_context(nc.sbuf_tensor("bq2", [128, J], F32))

        with ExitStack() as scope_x:
            xnT = scope_x.enter_context(nc.sbuf_tensor("xnT", [128, EC * T], BF16))

            with ExitStack() as p0:
                xt = p0.enter_context(nc.sbuf_tensor("xt", [128, EMB], F32))
                xsq = p0.enter_context(nc.sbuf_tensor("xsq", [128, EMB], F32))
                xn = p0.enter_context(nc.sbuf_tensor("xn", [128, EMB], BF16))
                stt = [p0.enter_context(nc.sbuf_tensor(f"stt{i}", [128, 8], F32)) for i in range(2)]
                pqf = p0.enter_context(nc.sbuf_tensor("pqf", [128, EMB], F32))
                prd = p0.enter_context(nc.sbuf_tensor("prd", [128, EMB // 4], F32))
                pjq = p0.enter_context(nc.sbuf_tensor("pjq", [128, 128 * J], BF16))
                skb = p0.enter_context(nc.sbuf_tensor("skb", [128, 128 * J], BF16))
                svb = p0.enter_context(nc.sbuf_tensor("svb", [128, 128 * J], BF16))
                pjk = p0.enter_context(nc.sbuf_tensor("pjk", [128, 128 * J], BF16))
                pjv = p0.enter_context(nc.sbuf_tensor("pjv", [128, 128 * J], BF16))
                ptx = [p0.enter_context(nc.psum_tensor(f"ptx{i}", [128, 512], BF16)) for i in range(2)]

                blk0 = p0.enter_context(nc.Block("p0"))

                @blk0.sync
                def _(s):
                    for i in range(NT0 if (p0mask & 1) else 0):
                        if i >= 1:
                            s.wait_ge(s_sc0.h, s_sc0.exp(2 * (i - 1) + 1))
                            s.wait_ge(s_ln.h, s_ln.exp(7 * (i - 1) + 7))
                        _inc(s.dma_start(xt[:, :], x_in[i * 128:(i + 1) * 128, :]), s_x)

                @blk0.scalar
                def _(sc):
                    for i in range(NT0 if (p0mask & 1) else 0):
                        p = i % 2
                        st = stt[p]
                        sc.wait_ge(s_x.h, s_x.exp(i + 1))
                        if i >= 2:
                            sc.wait_ge(s_ln.h, s_ln.exp(7 * (i - 2) + 7))
                        _inc(sc.activation(xsq[:, :], xt[:, :], AF.Square,
                                           accum_out=st[:, 1:2]), s_sc0)
                        sc.wait_ge(s_ln.h, s_ln.exp(7 * i + 5))
                        _inc(sc.activation(st[:, 6:7], st[:, 5:6], AF.Sqrt), s_sc0)

                @blk0.vector
                def _(v):
                    if not (p0mask & 8):
                        return
                    v.wait_ge(s_pk.h, s_pk.exp(3))
                    q4 = bqt[:, :].rearrange("p (h r) -> p h r", r=4)
                    _inc(v.tensor_add(bq2[:, :], q4[:, :, 0], q4[:, :, 1]), s_pv)
                    _inc(v.tensor_add(bqg[:, :], q4[:, :, 2], q4[:, :, 3]), s_pv)
                    v.wait_ge(s_pv.h, s_pv.exp(2))
                    _inc(v.tensor_add(bqg[:, :], bqg[:, :], bq2[:, :]), s_pv)
                    v.wait_ge(s_pv.h, s_pv.exp(3))
                    _inc(v.tensor_scalar_mul(bqg[:, :], bqg[:, :], SCALE), s_pv)
                    for i in range(NT0 if (p0mask & 1) else 0):
                        p = i % 2
                        st = stt[p]
                        v.wait_ge(s_x.h, s_x.exp(i + 1))
                        if i >= 2:
                            v.wait_ge(s_sc0.h, s_sc0.exp(2 * (i - 2) + 2))
                        _inc(v.reduce_sum(st[:, 0:1], xt[:, :], AX.X), s_ln)
                        v.wait_ge(s_sc0.h, s_sc0.exp(2 * i + 1))
                        v.wait_ge(s_ln.h, s_ln.exp(7 * i + 1))
                        _inc(v.tensor_scalar_mul(st[:, 2:3], st[:, 0:1], 1.0 / EMB), s_ln)
                        _inc(v.tensor_scalar_mul(st[:, 3:4], st[:, 1:2], 1.0 / EMB), s_ln)
                        v.wait_ge(s_ln.h, s_ln.exp(7 * i + 3))
                        _inc(v.tensor_mul(st[:, 4:5], st[:, 2:3], st[:, 2:3]), s_ln)
                        v.wait_ge(s_ln.h, s_ln.exp(7 * i + 4))
                        _inc(v.tensor_scalar(st[:, 5:6], st[:, 3:4], st[:, 4:5], 1e-5,
                                             ALU.subtract, ALU.add), s_ln)
                        v.wait_ge(s_sc0.h, s_sc0.exp(2 * i + 2))
                        _inc(v.reciprocal(st[:, 6:7], st[:, 6:7]), s_ln)
                        if i >= 1 and not (p0mask & 16):
                            for ec in range(EC):
                                gidx = (i - 1) * EC + ec
                                bank = (gidx // 4) % 2
                                slot = gidx % 4
                                v.wait_ge(s_xt.h, s_xt.exp(4 * (gidx // 4 + 1)))
                                _inc(v.tensor_copy(
                                    xnT[:, ec * T + (i - 1) * 128:ec * T + i * 128],
                                    ptx[bank][:, slot * 128:(slot + 1) * 128]), s_xe)
                        v.wait_ge(s_ln.h, s_ln.exp(7 * i + 6))
                        if i >= 1 and not (p0mask & 16):
                            v.wait_ge(s_xt.h, s_xt.exp(EC * i))
                        _inc(v.tensor_scalar(xn[:, :], xt[:, :], st[:, 2:3], st[:, 6:7],
                                             ALU.subtract, ALU.mult), s_ln)
                    for ec in range(EC if (p0mask & 1) and not (p0mask & 16) else 0):
                        gidx = (NT0 - 1) * EC + ec
                        bank = (gidx // 4) % 2
                        slot = gidx % 4
                        v.wait_ge(s_xt.h, s_xt.exp(4 * (gidx // 4 + 1)))
                        _inc(v.tensor_copy(
                            xnT[:, ec * T + (NT0 - 1) * 128:ec * T + NT0 * 128],
                            ptx[bank][:, slot * 128:(slot + 1) * 128]), s_xe)
                    for ec in range(EC if (p0mask & 4) else 0):
                        base = 4 + PVC * ec
                        v.wait_ge(s_pq.h, s_pq.exp(ec + 1))
                        if ec >= 1:
                            v.wait_ge(s_pw.h, s_pw.exp(3 * ec))
                        q4 = pqf[:, :].rearrange("p (c r) -> p c r", r=4)
                        _inc(v.tensor_add(prd[:, :], q4[:, :, 0], q4[:, :, 1]), s_pv)
                        v.wait_ge(s_pv.h, s_pv.exp(base + 1))
                        _inc(v.tensor_add(prd[:, :], prd[:, :], q4[:, :, 2]), s_pv)
                        v.wait_ge(s_pv.h, s_pv.exp(base + 2))
                        _inc(v.tensor_add(prd[:, :], prd[:, :], q4[:, :, 3]), s_pv)
                        v.wait_ge(s_pv.h, s_pv.exp(base + 3))
                        rj = prd[:, :].rearrange("p (d j) -> p d j", j=J)
                        for j in range(J):
                            _inc(v.tensor_scalar_mul(pjq[:, j * 128:(j + 1) * 128],
                                                     rj[:, :, j], SCALE), s_pv)
                        v.wait_ge(s_sk.h, s_sk.exp(ec + 1))
                        if ec >= 1:
                            v.wait_ge(s_pw.h, s_pw.exp(3 * ec))
                        rk = skb[:, :].rearrange("p (e j) -> p e j", j=J)
                        for j in range(J):
                            _inc(v.tensor_copy(pjk[:, j * 128:(j + 1) * 128], rk[:, :, j]), s_pv)
                        v.wait_ge(s_sv.h, s_sv.exp(ec + 1))
                        if ec >= 1:
                            v.wait_ge(s_pw.h, s_pw.exp(3 * ec))
                        rv = svb[:, :].rearrange("p (e j) -> p e j", j=J)
                        for j in range(J):
                            _inc(v.tensor_copy(pjv[:, j * 128:(j + 1) * 128], rv[:, :, j]), s_pv)

                @blk0.gpsimd
                def _(g):
                    _inc(g.memset(ident[:, :], 0.0), s_ini)
                    g.wait_ge(s_ini.h, 1)
                    _inc(g.affine_select(out=ident[:, :], in_=ident[:, :],
                                         compare_op=ALU.not_equal, fill=1.0, base=0,
                                         pattern=[[-1, 128]], channel_multiplier=1), s_ini)
                    _inc(g.dma_start(bqt[:, :], bq[:, :]), s_pk)
                    _inc(g.dma_start(bkc[:, :], bk[:, :]), s_pk)
                    _inc(g.dma_start(bvc[:, :], bv[:, :]), s_pk)
                    def _emit_writes(ec):
                        base = 4 + PVC * ec
                        if ec >= 1:
                            g.wait_ge(s_pw.h, s_pw.exp(3 * ec))
                        g.wait_ge(s_pv.h, s_pv.exp(base + 3 + J))
                        dst = bass.AP(wqgb.tensor, ec * 128 * 128,
                                      [[128, 128], [EMB * 128, J], [1, 128]])
                        _inc(g.dma_start(dst, pjq[:, :].rearrange("p (j d) -> p j d", d=128)), s_pw)
                        g.wait_ge(s_pv.h, s_pv.exp(base + 3 + 2 * J))
                        dst = bass.AP(wkb.tensor, ec * 128 * 128,
                                      [[128, 128], [EMB * 128, J], [1, 128]])
                        _inc(g.dma_start(dst, pjk[:, :].rearrange("p (j d) -> p j d", d=128)), s_pw)
                        g.wait_ge(s_pv.h, s_pv.exp(base + 3 + 3 * J))
                        dst = bass.AP(wvb.tensor, ec * 128 * 128,
                                      [[128, 128], [EMB * 128, J], [1, 128]])
                        _inc(g.dma_start(dst, pjv[:, :].rearrange("p (j d) -> p j d", d=128)), s_pw)
                    for ec in range(EC if (p0mask & 4) else 0):
                        if ec >= 1:
                            _emit_writes(ec - 1)
                            g.wait_ge(s_pv.h, s_pv.exp(4 + PVC * ec))
                        _inc(g.dma_start(pqf[:, :], wq[ec * 128:(ec + 1) * 128, :]), s_pq)
                        _inc(g.dma_start(skb[:, :], wk[ec * 128:(ec + 1) * 128, :]), s_sk)
                        _inc(g.dma_start(svb[:, :], wv[ec * 128:(ec + 1) * 128, :]), s_sv)
                    if p0mask & 4:
                        _emit_writes(EC - 1)

                @blk0.tensor
                def _(t):
                    t.wait_ge(s_ini.h, 2)
                    for i in range(NT0 if (p0mask & 1) and not (p0mask & 16) else 0):
                        t.wait_ge(s_ln.h, s_ln.exp(7 * i + 7))
                        for ec in range(EC):
                            gidx = i * EC + ec
                            grp = gidx // 4
                            bank = grp % 2
                            slot = gidx % 4
                            if slot == 0 and grp >= 2:
                                t.wait_ge(s_xe.h, s_xe.exp(4 * (grp - 1)))
                            _inc(t.transpose(ptx[bank][:, slot * 128:(slot + 1) * 128],
                                             xn[:, ec * 128:(ec + 1) * 128],
                                             ident[:, :]), s_xt)

            nc.all_engine_barrier()

            if phases < 1:
                return nc
            with ExitStack() as p1:
                wgT = [p1.enter_context(nc.sbuf_tensor(f"wgT{i}", [128, EC * 256], BF16)) for i in range(2)]
                wuT = [p1.enter_context(nc.sbuf_tensor(f"wuT{i}", [128, EC * 256], BF16)) for i in range(2)]
                sg = [p1.enter_context(nc.sbuf_tensor(f"sg{i}", [128, 512], F32)) for i in range(2)]
                ptt = [p1.enter_context(nc.sbuf_tensor(f"ptt{i}", [128, T], BF16)) for i in range(2)]
                pg = [p1.enter_context(nc.psum_tensor(f"pg{i}", [128, 512], F32)) for i in range(2)]
                pu = [p1.enter_context(nc.psum_tensor(f"pu{i}", [128, 512], F32)) for i in range(2)]

                blk1 = p1.enter_context(nc.Block("p1"))

                NP = NF // 2

                @blk1.sync
                def _(s):
                    for P in range(NP if (p1mask & 1) else 0):
                        s.wait_ge(s_cg.h, s_cg.exp(P + 1))
                        s.wait_ge(s_cu.h, s_cu.exp(P + 1))
                        if P >= 2 and (p1mask & 2):
                            s.wait_ge(s_mu.h, s_mu.exp(NQ1 * (2 * P - 2)))
                        if P >= 1:
                            s.wait_ge(s_tr.h, s_tr.exp(2 * EC * P))
                        for ec in range(EC):
                            _inc(s.dma_start_transpose(
                                wgT[P % 2][:, ec * 256:(ec + 1) * 256],
                                wgb[P * 256:(P + 1) * 256, ec * 128:(ec + 1) * 128]), s_tr)
                        for ec in range(EC):
                            _inc(s.dma_start_transpose(
                                wuT[P % 2][:, ec * 256:(ec + 1) * 256],
                                wub[P * 256:(P + 1) * 256, ec * 128:(ec + 1) * 128]), s_tr)

                @blk1.tensor
                def _(t):
                    for fc in range(NF if (p1mask & 2) else 0):
                        t.wait_ge(s_tr.h, s_tr.exp(2 * EC * (fc // 2 + 1)))
                        for tq in range(NQ1):
                            gi = fc * NQ1 + tq
                            par = gi % 2
                            if gi >= 2 and (p1mask & 4):
                                t.wait_ge(s_si.h, s_si.exp(gi - 1))
                            if gi >= 2 and (p1mask & 8):
                                t.wait_ge(s_ml.h, s_ml.exp(gi - 1))
                            fo = (fc % 2) * 128
                            for ec in range(EC):
                                i_ = t.matmul(pg[par][:, :],
                                              wgT[(fc // 2) % 2][:, ec * 256 + fo:ec * 256 + fo + 128],
                                              xnT[:, ec * T + tq * 512:ec * T + (tq + 1) * 512],
                                              start=(ec == 0), stop=(ec == EC - 1))
                                if ec == EC - 1:
                                    _inc(i_, s_mg)
                            for ec in range(EC):
                                i_ = t.matmul(pu[par][:, :],
                                              wuT[(fc // 2) % 2][:, ec * 256 + fo:ec * 256 + fo + 128],
                                              xnT[:, ec * T + tq * 512:ec * T + (tq + 1) * 512],
                                              start=(ec == 0), stop=(ec == EC - 1))
                                if ec == EC - 1:
                                    _inc(i_, s_mu)

                @blk1.scalar
                def _(sc):
                    for fc in range(NF if (p1mask & 4) else 0):
                        for tq in range(NQ1):
                            gi = fc * NQ1 + tq
                            par = gi % 2
                            sc.wait_ge(s_mg.h, s_mg.exp(gi + 1))
                            if gi >= 2 and (p1mask & 8):
                                sc.wait_ge(s_ml.h, s_ml.exp(gi - 1))
                            _inc(sc.activation(sg[par][:, :], pg[par][:, :], AF.Silu), s_si)

                @blk1.vector
                def _(v):
                    for fc in range(NF if (p1mask & 8) else 0):
                        for tq in range(NQ1):
                            gi = fc * NQ1 + tq
                            par = gi % 2
                            v.wait_ge(s_si.h, s_si.exp(gi + 1))
                            v.wait_ge(s_mu.h, s_mu.exp(gi + 1))
                            if fc >= 2 and tq == 0:
                                v.wait_ge(s_pt.h, s_pt.exp(fc - 1))
                            _inc(v.tensor_tensor(ptt[fc % 2][:, tq * 512:(tq + 1) * 512],
                                                 sg[par][:, :], pu[par][:, :], ALU.mult), s_ml)

                @blk1.gpsimd
                def _(g):
                    ncg = FF // 256
                    ncd = EMB // 256
                    q4l = EC * T // 4
                    nsp = 0
                    ncast = 0

                    def _cast(c):
                        if c >= 1:
                            g.wait_ge(s_cg.h, s_cg.exp(c))
                            g.wait_ge(s_cu.h, s_cu.exp(c))
                        _inc(g.dma_start(wgb[c * 256:(c + 1) * 256, :],
                                         wg[c * 256:(c + 1) * 256, :]), s_cg)
                        _inc(g.dma_start(wub[c * 256:(c + 1) * 256, :],
                                         wu[c * 256:(c + 1) * 256, :]), s_cu)

                    for c in range(min(3, ncg)):
                        _cast(c)
                        ncast = c + 1
                    nwd = 0
                    for fc in range(NF if (p1mask & 8) else 0):
                        if fc % 2 == 0 and ncast < ncg:
                            _cast(ncast)
                            ncast += 1
                        if fc % 2 == 1 and fc >= 11 and nwd < ncd:
                            d = nwd
                            if d >= 1:
                                g.wait_ge(s_cd.h, s_cd.exp(d))
                            _inc(g.dma_start(wdb[d * 256:(d + 1) * 256, :],
                                             wd[d * 256:(d + 1) * 256, :]), s_cd)
                            nwd += 1
                        if fc in (2, 4, 6, 8):
                            qq = nsp
                            if qq >= 1:
                                g.wait_ge(s_xb.h, s_xb.exp(qq))
                            _inc(g.dma_start(xnb[:, qq * q4l:(qq + 1) * q4l],
                                             xnT[:, qq * q4l:(qq + 1) * q4l]), s_xb)
                            nsp += 1
                        g.wait_ge(s_ml.h, s_ml.exp(NQ1 * (fc + 1)))
                        if fc >= 1:
                            g.wait_ge(s_pt.h, s_pt.exp(fc))
                        src = ptt[fc % 2][:, :].rearrange("p (tg t) -> p tg t", t=128)
                        dst = bass.AP(pts.tensor, fc * (T * 128),
                                      [[128, 128], [128 * 128, T // 128], [1, 128]])
                        _inc(g.dma_start(dst, src), s_pt)
                    while ncast < ncg:
                        _cast(ncast)
                        ncast += 1
                    while nsp < 4:
                        qq = nsp
                        if qq >= 1:
                            g.wait_ge(s_xb.h, s_xb.exp(qq))
                        _inc(g.dma_start(xnb[:, qq * q4l:(qq + 1) * q4l],
                                         xnT[:, qq * q4l:(qq + 1) * q4l]), s_xb)
                        nsp += 1
                    for d in range(nwd, ncd):
                        if d >= 1:
                            g.wait_ge(s_cd.h, s_cd.exp(d))
                        _inc(g.dma_start(wdb[d * 256:(d + 1) * 256, :],
                                         wd[d * 256:(d + 1) * 256, :]), s_cd)

            nc.all_engine_barrier()

        if phases < 2:
            return nc
        if True:
            with ExitStack() as p23:
                wcol = [p23.enter_context(nc.sbuf_tensor(f"wcol{i}", [128, EC * 128], BF16)) for i in range(2)]
                qgd = [p23.enter_context(nc.sbuf_tensor(f"qgd{i}", [128, 512], BF16)) for i in range(2)]
                qgt = p23.enter_context(nc.sbuf_tensor("qgt", [128, 2 * J * NSUB * 128], BF16))
                vall = p23.enter_context(nc.sbuf_tensor("vall", [128, (J + 1) * TCH], BF16))
                xnt2 = [p23.enter_context(nc.sbuf_tensor(f"xnt2_{i}", [128, EC * 512], BF16)) for i in range(2)]
                bq64 = [p23.enter_context(nc.sbuf_tensor(f"bq64_{i}", [J + 1, 64 * 128], BF16)) for i in range(2)]
                bk64 = [p23.enter_context(nc.sbuf_tensor(f"bk64_{i}", [J + 1, 64 * 128], BF16)) for i in range(2)]
                nm16 = [p23.enter_context(nc.sbuf_tensor(f"nm16_{i}", [128, 16], BF16)) for i in range(2)]
                nmr = [p23.enter_context(nc.sbuf_tensor(f"nmr{i}", [16, 128], BF16)) for i in range(2)]
                expt = [p23.enter_context(nc.sbuf_tensor(f"expt{i}", [128, 512], BF16)) for i in range(4)]
                sray = [p23.enter_context(nc.sbuf_tensor(f"sray{i}", [128, 8], F32)) for i in range(2)]
                avb = [p23.enter_context(nc.sbuf_tensor(f"avb{i}", [128, 64 * H], F32)) for i in range(2)]
                pqk = [p23.enter_context(nc.psum_tensor(f"pqk{i}", [128, 512], F32)) for i in range(2)]
                pt3 = [p23.enter_context(nc.psum_tensor(f"pt3{i}", [128, 512], F32)) for i in range(2)]
                pav = [p23.enter_context(nc.psum_tensor(f"pav{i}", [128, 512], F32)) for i in range(2)]
                ptr2 = [p23.enter_context(nc.psum_tensor(f"ptr2{i}", [128, 512], BF16)) for i in range(2)]

                blk2 = p23.enter_context(nc.Block("p23"))

                def OCG(tc, oc):
                    return tc * OCQ + oc

                def QKG(tc, oc):
                    return tc * 2 * J + oc

                def GRG(tc, s, g):
                    return (tc * NSUB + s) * 8 + g

                def CHG(tc, s, g, c):
                    return GRG(tc, s, g) * 4 + c

                def HFG(tc, s, g, hh):
                    return GRG(tc, s, g) * 2 + hh

                def B64(tc, s, half64):
                    return (tc * NSUB + s) * 2 + half64

                wsc = {0: wqgb, 1: wkb, 2: wvb}

                @blk2.gpsimd
                def _(g):
                    _inc(g.memset(bq64[0][:, :], 1.0), s_ini)
                    _inc(g.memset(bq64[1][:, :], 1.0), s_ini)
                    _inc(g.memset(bk64[0][:, :], 1.0), s_ini)
                    _inc(g.memset(bk64[1][:, :], 1.0), s_ini)
                    _inc(g.memset(vall[:, J * TCH:(J + 1) * TCH], 1.0), s_ini)
                    g.wait_ge(s_pw.h, s_pw.exp(3 * EC))
                    g.wait_ge(s_xb.h, s_xb.exp(4))
                    for tc in range(NTCH):
                        if tc >= 2:
                            g.wait_ge(s_m2.h, s_m2.exp((tc - 1) * OCQ))
                        if tc >= 1:
                            g.wait_ge(s_x2.h, s_x2.exp(tc))
                        src_ = bass.AP(xnb.tensor, tc * 512,
                                       [[EC * T, 128], [T, EC], [1, 512]])
                        _inc(g.dma_start(xnt2[tc % 2][:, :], src_), s_x2)
                        for oc in range(OCQ):
                            O = OCG(tc, oc)
                            if O >= 2:
                                g.wait_ge(s_m2.h, s_m2.exp(O - 1))
                            if O >= 1:
                                g.wait_ge(s_wc.h, s_wc.exp(O))
                            j = oc % J
                            src = bass.AP(wsc[oc // J].tensor, j * EMB * 128,
                                          [[128, 128], [128 * 128, EC], [1, 128]])
                            _inc(g.dma_start(
                                wcol[O % 2][:, :].rearrange("p (ec d) -> p ec d", d=128),
                                src), s_wc)

                @blk2.tensor
                def _(t):
                    t.wait_ge(s_ini.h, 7)
                    for tc in range(NTCH):
                        for oc in range(OCQ + 1):
                            if oc < OCQ:
                                O = OCG(tc, oc)
                                t.wait_ge(s_wc.h, s_wc.exp(O + 1))
                                if oc == 0:
                                    t.wait_ge(s_x2.h, s_x2.exp(tc + 1))
                                if O >= 2:
                                    t.wait_ge(s_e2.h, s_e2.exp(O - 1))
                                if oc < 2 and tc >= 1:
                                    t.wait_ge(s_rm.h, s_rm.exp(tc * NSUB * 32))
                                for ec in range(EC):
                                    i_ = t.matmul(pqk[O % 2][:, :],
                                                  wcol[O % 2][:, ec * 128:(ec + 1) * 128],
                                                  xnt2[tc % 2][:, ec * 512:(ec + 1) * 512],
                                                  start=(ec == 0), stop=(ec == EC - 1))
                                    if ec == EC - 1:
                                        _inc(i_, s_m2)
                            ocp = oc - 1
                            if 0 <= ocp < 2 * J:
                                Q = QKG(tc, ocp)
                                t.wait_ge(s_e2.h, s_e2.exp(OCG(tc, ocp) + 1))
                                if Q >= 2:
                                    t.wait_ge(s_te.h, s_te.exp(NSUB * (Q - 1)))
                                if ocp == 0 and tc >= 1:
                                    t.wait_ge(s_ne.h, s_ne.exp(tc * NGT))
                                for sub in range(NSUB):
                                    _inc(t.transpose(ptr2[Q % 2][:, sub * 128:(sub + 1) * 128],
                                                     qgd[Q % 2][:, sub * 128:(sub + 1) * 128],
                                                     ident[:, :]), s_t2)
                        for s in range(NSUB):
                            for g in range(8 + 2):
                                if g < 8:
                                    G = GRG(tc, s, g)
                                    if g % 4 == 0:
                                        HS = (tc * NSUB + s) * 2 + g // 4
                                        t.wait_ge(s_bl.h, s_bl.exp((HS + 1) * 2 * J))
                                    for c in range(4):
                                        C = CHG(tc, s, g, c)
                                        if C >= 2:
                                            t.wait_ge(s_rm.h, s_rm.exp(C - 1))
                                        if s == 0 and g == 0 and c < 2:
                                            t.wait_ge(s_e2.h, s_e2.exp(OCG(tc, OCQ - 1) + 1))
                                        for i in range(4):
                                            tt = g * 16 + c * 4 + i
                                            hs2 = tt // 64
                                            to = tt % 64
                                            mm = t.matmul(
                                                pqk[C % 2][:, i * 128:(i + 1) * 128],
                                                bq64[hs2 % 2][0:J, to * 128:(to + 1) * 128],
                                                bk64[hs2 % 2][0:J, to * 128:(to + 1) * 128],
                                                start=True, stop=True)
                                            if i == 3:
                                                _inc(mm, s_m1)
                                gp = g - 1
                                if 0 <= gp < 8:
                                    G = GRG(tc, s, gp)
                                    t.wait_ge(s_rm.h, s_rm.exp(4 * (G + 1)))
                                    if G >= 1:
                                        t.wait_ge(s_ne.h, s_ne.exp(G))
                                    _inc(t.transpose(ptr2[0][0:16, 0:128], nm16[G % 2][:, :],
                                                     ident[:, :]), s_nt)
                                gq = g - 2
                                if 0 <= gq < 8:
                                    G = GRG(tc, s, gq)
                                    for c in range(4):
                                        C = CHG(tc, s, gq, c)
                                        t.wait_ge(s_ex.h, s_ex.exp(C + 1))
                                        hh = c // 2
                                        HF = HFG(tc, s, gq, hh)
                                        if c % 2 == 0 and HF >= 2:
                                            t.wait_ge(s_no.h, s_no.exp(HF - 1))
                                        for i in range(4):
                                            tt = gq * 16 + c * 4 + i
                                            ttc = s * 128 + tt
                                            k = (c % 2) * 4 + i
                                            rhs = bass.AP(vall, ttc,
                                                          [[(J + 1) * TCH, 128], [TCH, J + 1], [0, 4]])
                                            mm = t.matmul(pav[HF % 2][:, k * VC:(k + 1) * VC],
                                                          expt[C % 4][:, i * 128:(i + 1) * 128],
                                                          rhs, start=True, stop=True)
                                            if i == 3:
                                                _inc(mm, s_m3)
                                if 0 <= gp < 8:
                                    G = GRG(tc, s, gp)
                                    t.wait_ge(s_in.h, s_in.exp(G + 1))
                                    for c in range(4):
                                        C = CHG(tc, s, gp, c)
                                        if C >= 2:
                                            t.wait_ge(s_ex.h, s_ex.exp(C - 1))
                                        for i in range(4):
                                            tt = gp * 16 + c * 4 + i
                                            hs2 = tt // 64
                                            to = tt % 64
                                            mm = t.matmul(
                                                pt3[C % 2][:, i * 128:(i + 1) * 128],
                                                bk64[hs2 % 2][0:J + 1, to * 128:(to + 1) * 128],
                                                bq64[hs2 % 2][0:J + 1, to * 128:(to + 1) * 128],
                                                start=True, stop=True)
                                            if i == 3:
                                                _inc(mm, s_mT)

                @blk2.vector
                def _(v):
                    for tc in range(NTCH):
                        for oc in range(OCQ):
                            O = OCG(tc, oc)
                            j = oc % J
                            v.wait_ge(s_m2.h, s_m2.exp(O + 1))
                            if oc < 2 * J:
                                Q = QKG(tc, oc)
                                if Q >= 2:
                                    v.wait_ge(s_t2.h, s_t2.exp(NSUB * (Q - 1)))
                                bias = bqg if oc < J else bkc
                                _inc(v.tensor_scalar(qgd[Q % 2][:, :], pqk[O % 2][:, :],
                                                     bias[:, j:j + 1], None, ALU.add), s_e2)
                            else:
                                if oc == 2 * J and tc >= 1:
                                    v.wait_ge(s_m3.h, s_m3.exp(tc * NSUB * 32))
                                _inc(v.tensor_scalar(vall[:, j * TCH:(j + 1) * TCH],
                                                     pqk[O % 2][:, :],
                                                     bvc[:, j:j + 1], None, ALU.add), s_e2)
                        for s in range(NSUB):
                            for g in range(8 + 2):
                                if g < 8:
                                    G = GRG(tc, s, g)
                                    for c in range(4):
                                        C = CHG(tc, s, g, c)
                                        v.wait_ge(s_m1.h, s_m1.exp(C + 1))
                                        if c == 0 and G >= 2:
                                            v.wait_ge(s_nt.h, s_nt.exp(G - 1))
                                        _inc(v.tensor_reduce(
                                            nm16[G % 2][:, c * 4:(c + 1) * 4],
                                            pqk[C % 2][:, :].rearrange("p (i e) -> p i e", i=4),
                                            AX.X, ALU.max, negate=True), s_rm)
                                gq = g - 2
                                if 0 <= gq < 8:
                                    G = GRG(tc, s, gq)
                                    for hh in range(2):
                                        HF = HFG(tc, s, gq, hh)
                                        v.wait_ge(s_m3.h, s_m3.exp(4 * G + 2 * hh + 2))
                                        rsum = bass.AP(pav[HF % 2], 4 * J,
                                                       [[512, 128], [VC, 8]])
                                        _inc(v.reciprocal(sray[HF % 2][:, :], rsum), s_rc)
                                        v.wait_ge(s_rc.h, s_rc.exp(HF + 1))
                                        hid = (gq * 2 + hh) % 8
                                        half64 = (gq * 2 + hh) // 8
                                        B = B64(tc, s, half64)
                                        if B >= 2 and hid == 0:
                                            v.wait_ge(s_as.h, s_as.exp(B - 1))
                                        for k in range(8):
                                            tok64 = hid * 8 + k
                                            ts_ = v.tensor_scalar(
                                                avb[B % 2][:, tok64 * H:(tok64 + 1) * H],
                                                pav[HF % 2][:, k * VC:k * VC + H],
                                                sray[HF % 2][:, k:k + 1], None, ALU.mult)
                                            if k == 7:
                                                _inc(ts_, s_no)

                @blk2.scalar
                def _(sc):
                    for tc in range(NTCH):
                        for ocp in range(2 * J):
                            Q = QKG(tc, ocp)
                            if ocp == 0 and tc >= 1:
                                sc.wait_ge(s_bl.h, s_bl.exp(tc * NSUB * 2 * J))
                            for sub in range(NSUB):
                                sc.wait_ge(s_t2.h, s_t2.exp(NSUB * (Q + 1)))
                                _inc(sc.copy(
                                    qgt[:, (ocp * NSUB + sub) * 128:(ocp * NSUB + sub + 1) * 128],
                                    ptr2[Q % 2][:, sub * 128:(sub + 1) * 128]), s_te)
                        for s in range(NSUB):
                            for g in range(8 + 2):
                                gp = g - 1
                                if 0 <= gp < 8:
                                    G = GRG(tc, s, gp)
                                    sc.wait_ge(s_nt.h, s_nt.exp(G + 1))
                                    if G >= 2:
                                        sc.wait_ge(s_in.h, s_in.exp(G - 1))
                                    _inc(sc.copy(nmr[G % 2][:, :], ptr2[0][0:16, 0:128]), s_ne)
                                    for c in range(4):
                                        C = CHG(tc, s, gp, c)
                                        sc.wait_ge(s_mT.h, s_mT.exp(C + 1))
                                        if C >= 4:
                                            sc.wait_ge(s_m3.h, s_m3.exp(C - 3))
                                        _inc(sc.activation(expt[C % 4][:, :], pt3[C % 2][:, :],
                                                           AF.Exp), s_ex)
                            for half64 in range(2):
                                B = B64(tc, s, half64)
                                sc.wait_ge(s_no.h, s_no.exp((B + 1) * 8))
                                if B >= 1:
                                    sc.wait_ge(s_as.h, s_as.exp(B))
                                base = tc * TCH + s * 128 + half64 * 64
                                dst = bass.AP(attn_s.tensor, base * EMB,
                                              [[H, 128], [EMB, 64], [1, H]])
                                _inc(sc.dma_start(dst, avb[B % 2][:, :].rearrange(
                                    "p (t h) -> p t h", h=H)), s_as)

                @blk2.sync
                def _(s):
                    for tc in range(NTCH):
                        for sub in range(NSUB):
                            SB = tc * NSUB + sub
                            for hs in range(2):
                                HS = SB * 2 + hs
                                s.wait_ge(s_te.h, s_te.exp((tc * 2 * J + 2 * J) * NSUB))
                                if HS >= 1:
                                    s.wait_ge(s_bl.h, s_bl.exp(HS * 2 * J))
                                if HS >= 2:
                                    s.wait_ge(s_mT.h, s_mT.exp((HS - 1) * 16))
                                for j in range(J):
                                    src = qgt[hs * 64:hs * 64 + 64,
                                              (j * NSUB + sub) * 128:(j * NSUB + sub + 1) * 128]
                                    dst = bass.AP(bq64[hs % 2], j * 64 * 128,
                                                  [[64 * 128, 1], [128, 64], [1, 128]])
                                    _inc(s.dma_start(dst, src), s_bl)
                                for j in range(J):
                                    src = qgt[hs * 64:hs * 64 + 64,
                                              ((J + j) * NSUB + sub) * 128:((J + j) * NSUB + sub + 1) * 128]
                                    dst = bass.AP(bk64[hs % 2], j * 64 * 128,
                                                  [[64 * 128, 1], [128, 64], [1, 128]])
                                    _inc(s.dma_start(dst, src), s_bl)
                            for g in range(8):
                                G = GRG(tc, sub, g)
                                s.wait_ge(s_ne.h, s_ne.exp(G + 1))
                                if G >= 1:
                                    s.wait_ge(s_in.h, s_in.exp(G))
                                dst = bass.AP(bq64[(g // 4) % 2],
                                              J * 64 * 128 + (g % 4) * 16 * 128,
                                              [[64 * 128, 1], [128, 16], [1, 128]])
                                _inc(s.dma_start(dst, nmr[G % 2][:, :]), s_in)

            nc.all_engine_barrier()

        if phases < 3:
            return nc
        with ExitStack() as p4:
            wdT = p4.enter_context(nc.sbuf_tensor("wdT", [128, NF * EP], BF16))
            ptsb = [p4.enter_context(nc.sbuf_tensor(f"ptsb{i}", [128, NF * 128], BF16)) for i in range(2)]
            xsl = [p4.enter_context(nc.sbuf_tensor(f"xsl{i}", [128, EP], F32)) for i in range(2)]
            asl = [p4.enter_context(nc.sbuf_tensor(f"asl{i}", [128, EP], F32)) for i in range(2)]
            osb = [p4.enter_context(nc.sbuf_tensor(f"osb{i}", [128, EP], F32)) for i in range(2)]
            py = [p4.enter_context(nc.psum_tensor(f"py{i}", [128, EP], F32)) for i in range(2)]

            blk4 = p4.enter_context(nc.Block("p4"))

            @blk4.sync
            def _(s):
                s.wait_ge(s_cd.h, s_cd.exp(EMB // 256))
                WB = 8
                for p in range(NEP):
                    if p >= 1:
                        s.wait_ge(s_my.h, s_my.exp(p * NTG))
                    for fc in range(NF):
                        if fc % WB == 0 and p * NF + fc > 0:
                            s.wait_ge(s_wt.h, s_wt.exp(p * NF + fc))
                        _inc(s.dma_start_transpose(
                            wdT[:, fc * EP:(fc + 1) * EP],
                            wdb[p * EP:(p + 1) * EP, fc * 128:(fc + 1) * 128]), s_wt)
                    for tg in range(NTG):
                        PTG = p * NTG + tg
                        s.wait_ge(s_cb.h, s_cb.exp(2 * (PTG + 1)))
                        if PTG >= 1:
                            s.wait_ge(s_ou.h, s_ou.exp(PTG))
                        _inc(s.dma_start(out[tg * 128:(tg + 1) * 128, p * EP:(p + 1) * EP],
                                         osb[PTG % 2][:, :]), s_ou)
                s.wait_ge(s_ou.h, s_ou.exp(NEP * NTG))

            @blk4.gpsimd
            def _(g):
                g.wait_ge(s_pt.h, s_pt.exp(NF))
                NH = NF // 2
                for p in range(NEP):
                    for tg in range(NTG):
                        PTG = p * NTG + tg
                        if PTG >= 2:
                            g.wait_ge(s_my.h, s_my.exp(PTG - 1))
                        if PTG >= 1:
                            g.wait_ge(s_p4.h, s_p4.exp(2 * PTG))
                        for hf in range(2):
                            src = bass.AP(pts.tensor, hf * NH * T * 128 + tg * 128 * 128,
                                          [[128, 128], [T * 128, NH], [1, 128]])
                            dst = ptsb[PTG % 2][:, hf * NH * 128:(hf + 1) * NH * 128].rearrange(
                                "p (f t) -> p f t", t=128)
                            _inc(g.dma_start(dst, src), s_p4)

            @blk4.scalar
            def _(sc):
                sc.wait_ge(s_as.h, s_as.exp(T // 64))
                for p in range(NEP):
                    for tg in range(NTG):
                        PTG = p * NTG + tg
                        if PTG >= 2:
                            sc.wait_ge(s_cb.h, s_cb.exp(2 * (PTG - 1)))
                        if PTG >= 1:
                            sc.wait_ge(s_xa.h, s_xa.exp(2 * PTG))
                        _inc(sc.dma_start(xsl[PTG % 2][:, :],
                                          x_in[tg * 128:(tg + 1) * 128, p * EP:(p + 1) * EP]), s_xa)
                        _inc(sc.dma_start(asl[PTG % 2][:, :],
                                          attn_s[tg * 128:(tg + 1) * 128, p * EP:(p + 1) * EP]), s_xa)

            @blk4.tensor
            def _(t):
                for p in range(NEP):
                    for tg in range(NTG):
                        PTG = p * NTG + tg
                        t.wait_ge(s_p4.h, s_p4.exp(2 * (PTG + 1)))
                        if PTG >= 2:
                            t.wait_ge(s_cb.h, s_cb.exp(2 * (PTG - 1)))
                        for fc in range(NF):
                            if tg == 0 and fc % 8 == 0:
                                t.wait_ge(s_wt.h, s_wt.exp(p * NF + min(NF, fc + 8)))
                            i_ = t.matmul(py[PTG % 2][:, :],
                                          ptsb[PTG % 2][:, fc * 128:(fc + 1) * 128],
                                          wdT[:, fc * EP:(fc + 1) * EP],
                                          start=(fc == 0), stop=(fc == NF - 1))
                            if fc == NF - 1:
                                _inc(i_, s_my)

            @blk4.vector
            def _(v):
                for p in range(NEP):
                    for tg in range(NTG):
                        PTG = p * NTG + tg
                        v.wait_ge(s_my.h, s_my.exp(PTG + 1))
                        v.wait_ge(s_xa.h, s_xa.exp(2 * (PTG + 1)))
                        if PTG >= 2:
                            v.wait_ge(s_ou.h, s_ou.exp(PTG - 1))
                        _inc(v.tensor_add(osb[PTG % 2][:, :], xsl[PTG % 2][:, :],
                                          py[PTG % 2][:, :]), s_cb)
                        v.wait_ge(s_cb.h, s_cb.exp(2 * PTG + 1))
                        _inc(v.tensor_add(osb[PTG % 2][:, :], osb[PTG % 2][:, :],
                                          asl[PTG % 2][:, :]), s_cb)

    return nc


_PROG_CACHE = {}


def _get_prog(T, EMB, FF):
    key = (T, EMB, FF)
    if key not in _PROG_CACHE:
        _PROG_CACHE[key] = build_program(T, EMB, FF)
    return _PROG_CACHE[key]


def kernel(L0_attn_x, L0_attn_Wq, L0_attn_bq, L0_attn_Wk, L0_attn_bk,
           L0_attn_Wv, L0_attn_bv, L0_ffn_W_gate, L0_ffn_W_up, L0_ffn_W_down):
    x = np.ascontiguousarray(np.asarray(L0_attn_x, dtype=np.float32))
    B, S, EMB = x.shape
    NTOK = B * S
    T = NTOK // N_CORES
    x2 = x.reshape(NTOK, EMB)
    wq2 = np.ascontiguousarray(np.asarray(L0_attn_Wq, dtype=np.float32).reshape(EMB, -1))
    wk2 = np.ascontiguousarray(np.asarray(L0_attn_Wk, dtype=np.float32).reshape(EMB, -1))
    wv2 = np.ascontiguousarray(np.asarray(L0_attn_Wv, dtype=np.float32).reshape(EMB, -1))
    bq2 = np.ascontiguousarray(np.asarray(L0_attn_bq, dtype=np.float32))
    bk2 = np.ascontiguousarray(np.asarray(L0_attn_bk, dtype=np.float32))
    bv2 = np.ascontiguousarray(np.asarray(L0_attn_bv, dtype=np.float32))
    wg2 = np.ascontiguousarray(np.asarray(L0_ffn_W_gate, dtype=np.float32))
    wu2 = np.ascontiguousarray(np.asarray(L0_ffn_W_up, dtype=np.float32))
    wd2 = np.ascontiguousarray(np.asarray(L0_ffn_W_down, dtype=np.float32))
    FF = wg2.shape[0]

    nc = _get_prog(T, EMB, FF)
    in_maps = []
    for c in range(N_CORES):
        in_maps.append({
            "x": x2[c * T:(c + 1) * T],
            "wq": wq2, "wk": wk2, "wv": wv2,
            "bq": bq2, "bk": bk2, "bv": bv2,
            "wg": wg2, "wu": wu2, "wd": wd2,
        })
    res = run_bass_kernel_spmd(nc, in_maps, core_ids=list(range(N_CORES)))
    outs = [np.asarray(res.results[c]["out"]) for c in range(N_CORES)]
    return np.concatenate(outs, axis=0).reshape(B, S, EMB).astype(np.float32)


# version 08
# speedup vs baseline: 1.0591x; 1.0591x over previous
import numpy as np
import ml_dtypes
from contextlib import ExitStack

import concourse.bass as bass
import concourse.mybir as mybir
from concourse.bass_utils import run_bass_kernel_spmd

F32 = mybir.dt.float32
BF16 = mybir.dt.bfloat16
AF = mybir.ActivationFunctionType
ALU = mybir.AluOpType
AX = mybir.AxisListType

N_CORES = 8


class SemT:
    def __init__(self, handle, unit):
        self.h = handle
        self.unit = unit

    def exp(self, k):
        return k * self.unit


def _inc(inst, sem):
    inst.then_inc(sem.h, sem.unit)
    return inst


def build_program(T=2048, EMB=4096, FF=11008, detect_races=False, phases=4, p0mask=15, p1mask=15):
    assert EMB % 512 == 0 and T % 512 == 0 and FF % 256 == 0
    EC = EMB // 128
    H = EMB // 128
    J = H // 4
    NF = FF // 128
    TCH = 512
    NTCH = T // TCH
    NSUB = TCH // 128
    NQ1 = T // 512
    NT0 = T // 128
    NTG = T // 128
    EP = 512
    NEP = EMB // EP
    NGT = NSUB * 8
    VC = 4 * (J + 1)
    SCALE = 1.0 / float(np.sqrt(128.0))
    OCQ = 3 * J
    PVC = 3 + 3 * J

    nc = bass.Bass("TRN2", target_bir_lowering=False,
                   detect_race_conditions=detect_races)

    x_in = nc.dram_tensor("x", [T, EMB], F32, kind="ExternalInput").ap()
    wq = nc.dram_tensor("wq", [EMB, EMB], F32, kind="ExternalInput").ap()
    wk = nc.dram_tensor("wk", [EMB, 128 * J], F32, kind="ExternalInput").ap()
    wv = nc.dram_tensor("wv", [EMB, 128 * J], F32, kind="ExternalInput").ap()
    bq = nc.dram_tensor("bq", [128, H], F32, kind="ExternalInput").ap()
    bk = nc.dram_tensor("bk", [128, J], F32, kind="ExternalInput").ap()
    bv = nc.dram_tensor("bv", [128, J], F32, kind="ExternalInput").ap()
    wg = nc.dram_tensor("wg", [FF, EMB], F32, kind="ExternalInput").ap()
    wu = nc.dram_tensor("wu", [FF, EMB], F32, kind="ExternalInput").ap()
    wd = nc.dram_tensor("wd", [EMB, FF], F32, kind="ExternalInput").ap()
    out = nc.dram_tensor("out", [T, EMB], F32, kind="ExternalOutput").ap()

    wgb = nc.dram_tensor("wgb", [FF, EMB], BF16, kind="Internal").ap()
    wub = nc.dram_tensor("wub", [FF, EMB], BF16, kind="Internal").ap()
    wdb = nc.dram_tensor("wdb", [EMB, FF], BF16, kind="Internal").ap()
    wqgb = nc.dram_tensor("wqgb", [J * EMB * 128], BF16, kind="Internal").ap()
    wkb = nc.dram_tensor("wkb", [J * EMB * 128], BF16, kind="Internal").ap()
    wvb = nc.dram_tensor("wvb", [J * EMB * 128], BF16, kind="Internal").ap()
    pts = nc.dram_tensor("pts", [NF * T * 128], BF16, kind="Internal").ap()
    xnb = nc.dram_tensor("xnb", [128, EC * T], BF16, kind="Internal").ap()
    attn_s = nc.dram_tensor("attn_s", [T, EMB], F32, kind="Internal").ap()

    with ExitStack() as root:
        def SEM(name, unit=1):
            return SemT(root.enter_context(nc.semaphore(name)), unit)

        s_x = SEM("s_x", 16)
        s_ln = SEM("s_ln")
        s_sc0 = SEM("s_sc0")
        s_xt = SEM("s_xt")
        s_xe = SEM("s_xe")
        s_cg = SEM("s_cg", 16)
        s_cu = SEM("s_cu", 16)
        s_cd = SEM("s_cd", 16)
        s_pq = SEM("s_pq", 16)
        s_pk = SEM("s_pk", 16)
        s_sk = SEM("s_sk", 16)
        s_sv = SEM("s_sv", 16)
        s_pv = SEM("s_pv")
        s_pw = SEM("s_pw", 16)
        s_ini = SEM("s_ini")
        s_tr = SEM("s_tr", 16)
        s_mg = SEM("s_mg")
        s_mu = SEM("s_mu")
        s_mf = SEM("s_mf")
        s_si = SEM("s_si")
        s_ml = SEM("s_ml")
        s_pt = SEM("s_pt", 16)
        s_xb = SEM("s_xb", 16)
        s_x2 = SEM("s_x2", 16)
        s_wc = SEM("s_wc", 16)
        s_m2 = SEM("s_m2")
        s_e2 = SEM("s_e2")
        s_t2 = SEM("s_t2")
        s_te = SEM("s_te")
        s_bl = SEM("s_bl", 16)
        s_m1 = SEM("s_m1")
        s_rm = SEM("s_rm")
        s_nt = SEM("s_nt")
        s_ne = SEM("s_ne")
        s_in = SEM("s_in", 16)
        s_mT = SEM("s_mT")
        s_ex = SEM("s_ex")
        s_m3 = SEM("s_m3")
        s_rc = SEM("s_rc")
        s_no = SEM("s_no")
        s_as = SEM("s_as", 16)
        s_wt = SEM("s_wt", 16)
        s_p4 = SEM("s_p4", 16)
        s_xa = SEM("s_xa", 16)
        s_my = SEM("s_my")
        s_cb = SEM("s_cb")
        s_ou = SEM("s_ou", 16)

        ident = root.enter_context(nc.sbuf_tensor("ident", [128, 128], BF16))
        bqg = root.enter_context(nc.sbuf_tensor("bqg", [128, J], F32))
        bkc = root.enter_context(nc.sbuf_tensor("bkc", [128, J], F32))
        bvc = root.enter_context(nc.sbuf_tensor("bvc", [128, J], F32))
        bqt = root.enter_context(nc.sbuf_tensor("bqt", [128, H], F32))
        bq2 = root.enter_context(nc.sbuf_tensor("bq2", [128, J], F32))

        with ExitStack() as scope_x:
            xnT = scope_x.enter_context(nc.sbuf_tensor("xnT", [128, EC * T], BF16))

            with ExitStack() as p0:
                xt = p0.enter_context(nc.sbuf_tensor("xt", [128, EMB], F32))
                xsq = p0.enter_context(nc.sbuf_tensor("xsq", [128, EMB], F32))
                xn = p0.enter_context(nc.sbuf_tensor("xn", [128, EMB], BF16))
                stt = [p0.enter_context(nc.sbuf_tensor(f"stt{i}", [128, 8], F32)) for i in range(2)]
                pqf = p0.enter_context(nc.sbuf_tensor("pqf", [128, EMB], F32))
                prd = p0.enter_context(nc.sbuf_tensor("prd", [128, EMB // 4], F32))
                pjq = p0.enter_context(nc.sbuf_tensor("pjq", [128, 128 * J], BF16))
                skb = p0.enter_context(nc.sbuf_tensor("skb", [128, 128 * J], BF16))
                svb = p0.enter_context(nc.sbuf_tensor("svb", [128, 128 * J], BF16))
                pjk = p0.enter_context(nc.sbuf_tensor("pjk", [128, 128 * J], BF16))
                pjv = p0.enter_context(nc.sbuf_tensor("pjv", [128, 128 * J], BF16))
                ptx = [p0.enter_context(nc.psum_tensor(f"ptx{i}", [128, 512], BF16)) for i in range(2)]

                blk0 = p0.enter_context(nc.Block("p0"))

                @blk0.sync
                def _(s):
                    for i in range(NT0 if (p0mask & 1) else 0):
                        if i >= 1:
                            s.wait_ge(s_sc0.h, s_sc0.exp(2 * (i - 1) + 1))
                            s.wait_ge(s_ln.h, s_ln.exp(7 * (i - 1) + 7))
                        _inc(s.dma_start(xt[:, :], x_in[i * 128:(i + 1) * 128, :]), s_x)

                @blk0.scalar
                def _(sc):
                    for i in range(NT0 if (p0mask & 1) else 0):
                        p = i % 2
                        st = stt[p]
                        sc.wait_ge(s_x.h, s_x.exp(i + 1))
                        if i >= 2:
                            sc.wait_ge(s_ln.h, s_ln.exp(7 * (i - 2) + 7))
                        _inc(sc.activation(xsq[:, :], xt[:, :], AF.Square,
                                           accum_out=st[:, 1:2]), s_sc0)
                        sc.wait_ge(s_ln.h, s_ln.exp(7 * i + 5))
                        _inc(sc.activation(st[:, 6:7], st[:, 5:6], AF.Sqrt), s_sc0)

                @blk0.vector
                def _(v):
                    if not (p0mask & 8):
                        return
                    v.wait_ge(s_pk.h, s_pk.exp(3))
                    q4 = bqt[:, :].rearrange("p (h r) -> p h r", r=4)
                    _inc(v.tensor_add(bq2[:, :], q4[:, :, 0], q4[:, :, 1]), s_pv)
                    _inc(v.tensor_add(bqg[:, :], q4[:, :, 2], q4[:, :, 3]), s_pv)
                    v.wait_ge(s_pv.h, s_pv.exp(2))
                    _inc(v.tensor_add(bqg[:, :], bqg[:, :], bq2[:, :]), s_pv)
                    v.wait_ge(s_pv.h, s_pv.exp(3))
                    _inc(v.tensor_scalar_mul(bqg[:, :], bqg[:, :], SCALE), s_pv)
                    for i in range(NT0 if (p0mask & 1) else 0):
                        p = i % 2
                        st = stt[p]
                        v.wait_ge(s_x.h, s_x.exp(i + 1))
                        if i >= 2:
                            v.wait_ge(s_sc0.h, s_sc0.exp(2 * (i - 2) + 2))
                        _inc(v.reduce_sum(st[:, 0:1], xt[:, :], AX.X), s_ln)
                        v.wait_ge(s_sc0.h, s_sc0.exp(2 * i + 1))
                        v.wait_ge(s_ln.h, s_ln.exp(7 * i + 1))
                        _inc(v.tensor_scalar_mul(st[:, 2:3], st[:, 0:1], 1.0 / EMB), s_ln)
                        _inc(v.tensor_scalar_mul(st[:, 3:4], st[:, 1:2], 1.0 / EMB), s_ln)
                        v.wait_ge(s_ln.h, s_ln.exp(7 * i + 3))
                        _inc(v.tensor_mul(st[:, 4:5], st[:, 2:3], st[:, 2:3]), s_ln)
                        v.wait_ge(s_ln.h, s_ln.exp(7 * i + 4))
                        _inc(v.tensor_scalar(st[:, 5:6], st[:, 3:4], st[:, 4:5], 1e-5,
                                             ALU.subtract, ALU.add), s_ln)
                        v.wait_ge(s_sc0.h, s_sc0.exp(2 * i + 2))
                        _inc(v.reciprocal(st[:, 6:7], st[:, 6:7]), s_ln)
                        if i >= 1 and not (p0mask & 16):
                            for ec in range(EC):
                                gidx = (i - 1) * EC + ec
                                bank = (gidx // 4) % 2
                                slot = gidx % 4
                                v.wait_ge(s_xt.h, s_xt.exp(4 * (gidx // 4 + 1)))
                                _inc(v.tensor_copy(
                                    xnT[:, ec * T + (i - 1) * 128:ec * T + i * 128],
                                    ptx[bank][:, slot * 128:(slot + 1) * 128]), s_xe)
                        v.wait_ge(s_ln.h, s_ln.exp(7 * i + 6))
                        if i >= 1 and not (p0mask & 16):
                            v.wait_ge(s_xt.h, s_xt.exp(EC * i))
                        _inc(v.tensor_scalar(xn[:, :], xt[:, :], st[:, 2:3], st[:, 6:7],
                                             ALU.subtract, ALU.mult), s_ln)
                    for ec in range(EC if (p0mask & 1) and not (p0mask & 16) else 0):
                        gidx = (NT0 - 1) * EC + ec
                        bank = (gidx // 4) % 2
                        slot = gidx % 4
                        v.wait_ge(s_xt.h, s_xt.exp(4 * (gidx // 4 + 1)))
                        _inc(v.tensor_copy(
                            xnT[:, ec * T + (NT0 - 1) * 128:ec * T + NT0 * 128],
                            ptx[bank][:, slot * 128:(slot + 1) * 128]), s_xe)
                    for ec in range(EC if (p0mask & 4) else 0):
                        base = 4 + PVC * ec
                        v.wait_ge(s_pq.h, s_pq.exp(ec + 1))
                        if ec >= 1:
                            v.wait_ge(s_pw.h, s_pw.exp(3 * ec))
                        q4 = pqf[:, :].rearrange("p (c r) -> p c r", r=4)
                        _inc(v.tensor_add(prd[:, :], q4[:, :, 0], q4[:, :, 1]), s_pv)
                        v.wait_ge(s_pv.h, s_pv.exp(base + 1))
                        _inc(v.tensor_add(prd[:, :], prd[:, :], q4[:, :, 2]), s_pv)
                        v.wait_ge(s_pv.h, s_pv.exp(base + 2))
                        _inc(v.tensor_add(prd[:, :], prd[:, :], q4[:, :, 3]), s_pv)
                        v.wait_ge(s_pv.h, s_pv.exp(base + 3))
                        rj = prd[:, :].rearrange("p (d j) -> p d j", j=J)
                        for j in range(J):
                            _inc(v.tensor_scalar_mul(pjq[:, j * 128:(j + 1) * 128],
                                                     rj[:, :, j], SCALE), s_pv)
                        v.wait_ge(s_sk.h, s_sk.exp(ec + 1))
                        rk = skb[:, :].rearrange("p (e j) -> p e j", j=J)
                        for j in range(J):
                            _inc(v.tensor_copy(pjk[:, j * 128:(j + 1) * 128], rk[:, :, j]), s_pv)
                        v.wait_ge(s_sv.h, s_sv.exp(ec + 1))
                        rv = svb[:, :].rearrange("p (e j) -> p e j", j=J)
                        for j in range(J):
                            _inc(v.tensor_copy(pjv[:, j * 128:(j + 1) * 128], rv[:, :, j]), s_pv)

                @blk0.gpsimd
                def _(g):
                    _inc(g.memset(ident[:, :], 0.0), s_ini)
                    g.wait_ge(s_ini.h, 1)
                    _inc(g.affine_select(out=ident[:, :], in_=ident[:, :],
                                         compare_op=ALU.not_equal, fill=1.0, base=0,
                                         pattern=[[-1, 128]], channel_multiplier=1), s_ini)
                    _inc(g.dma_start(bqt[:, :], bq[:, :]), s_pk)
                    _inc(g.dma_start(bkc[:, :], bk[:, :]), s_pk)
                    _inc(g.dma_start(bvc[:, :], bv[:, :]), s_pk)
                    def _emit_writes(ec):
                        base = 4 + PVC * ec
                        if ec >= 1:
                            g.wait_ge(s_pw.h, s_pw.exp(3 * ec))
                        g.wait_ge(s_pv.h, s_pv.exp(base + 3 + J))
                        dst = bass.AP(wqgb.tensor, ec * 128 * 128,
                                      [[128, 128], [EMB * 128, J], [1, 128]])
                        _inc(g.dma_start(dst, pjq[:, :].rearrange("p (j d) -> p j d", d=128)), s_pw)
                        g.wait_ge(s_pv.h, s_pv.exp(base + 3 + 2 * J))
                        dst = bass.AP(wkb.tensor, ec * 128 * 128,
                                      [[128, 128], [EMB * 128, J], [1, 128]])
                        _inc(g.dma_start(dst, pjk[:, :].rearrange("p (j d) -> p j d", d=128)), s_pw)
                        g.wait_ge(s_pv.h, s_pv.exp(base + 3 + 3 * J))
                        dst = bass.AP(wvb.tensor, ec * 128 * 128,
                                      [[128, 128], [EMB * 128, J], [1, 128]])
                        _inc(g.dma_start(dst, pjv[:, :].rearrange("p (j d) -> p j d", d=128)), s_pw)
                    for ec in range(EC if (p0mask & 4) else 0):
                        if ec >= 1:
                            _emit_writes(ec - 1)
                        _inc(g.dma_start(pqf[:, :], wq[ec * 128:(ec + 1) * 128, :]), s_pq)
                        _inc(g.dma_start(skb[:, :], wk[ec * 128:(ec + 1) * 128, :]), s_sk)
                        _inc(g.dma_start(svb[:, :], wv[ec * 128:(ec + 1) * 128, :]), s_sv)
                        g.wait_ge(s_pv.h, s_pv.exp(4 + PVC * (ec + 1)) if ec + 1 < EC else 0)
                    if p0mask & 4:
                        _emit_writes(EC - 1)

                @blk0.tensor
                def _(t):
                    t.wait_ge(s_ini.h, 2)
                    for i in range(NT0 if (p0mask & 1) and not (p0mask & 16) else 0):
                        t.wait_ge(s_ln.h, s_ln.exp(7 * i + 7))
                        for ec in range(EC):
                            gidx = i * EC + ec
                            grp = gidx // 4
                            bank = grp % 2
                            slot = gidx % 4
                            if slot == 0 and grp >= 2:
                                t.wait_ge(s_xe.h, s_xe.exp(4 * (grp - 1)))
                            _inc(t.transpose(ptx[bank][:, slot * 128:(slot + 1) * 128],
                                             xn[:, ec * 128:(ec + 1) * 128],
                                             ident[:, :]), s_xt)

            nc.all_engine_barrier()

            if phases < 1:
                return nc
            with ExitStack() as p1:
                wgT = [p1.enter_context(nc.sbuf_tensor(f"wgT{i}", [128, EC * 256], BF16)) for i in range(2)]
                wuT = [p1.enter_context(nc.sbuf_tensor(f"wuT{i}", [128, EC * 256], BF16)) for i in range(2)]
                sg = [p1.enter_context(nc.sbuf_tensor(f"sg{i}", [128, 512], F32)) for i in range(2)]
                ptt = [p1.enter_context(nc.sbuf_tensor(f"ptt{i}", [128, T], BF16)) for i in range(2)]
                pg = [p1.enter_context(nc.psum_tensor(f"pg{i}", [128, 512], F32)) for i in range(2)]
                pu = [p1.enter_context(nc.psum_tensor(f"pu{i}", [128, 512], F32)) for i in range(2)]

                blk1 = p1.enter_context(nc.Block("p1"))

                NP = NF // 2

                def _trans_pair(eng, P, wT, wb_, csem):
                    eng.wait_ge(csem.h, csem.exp(P + 1))
                    if P >= 2 and (p1mask & 2):
                        eng.wait_ge(s_mu.h, s_mu.exp(NQ1 * (2 * P - 2)))
                    if P >= 1:
                        eng.wait_ge(s_tr.h, s_tr.exp(2 * EC * P))
                    for ec in range(EC):
                        _inc(eng.dma_start_transpose(
                            wT[P % 2][:, ec * 256:(ec + 1) * 256],
                            wb_[P * 256:(P + 1) * 256, ec * 128:(ec + 1) * 128]), s_tr)

                @blk1.sync
                def _(s):
                    for P in range(NP if (p1mask & 1) else 0):
                        _trans_pair(s, P, wgT, wgb, s_cg)
                        _trans_pair(s, P, wuT, wub, s_cu)

                @blk1.tensor
                def _(t):
                    for fc in range(NF if (p1mask & 2) else 0):
                        t.wait_ge(s_tr.h, s_tr.exp(2 * EC * (fc // 2 + 1)))
                        for tq in range(NQ1):
                            gi = fc * NQ1 + tq
                            par = gi % 2
                            if gi >= 2 and (p1mask & 4):
                                t.wait_ge(s_si.h, s_si.exp(gi - 1))
                            if gi >= 2 and (p1mask & 8):
                                t.wait_ge(s_ml.h, s_ml.exp(gi - 1))
                            fo = (fc % 2) * 128
                            for ec in range(EC):
                                i_ = t.matmul(pg[par][:, :],
                                              wgT[(fc // 2) % 2][:, ec * 256 + fo:ec * 256 + fo + 128],
                                              xnT[:, ec * T + tq * 512:ec * T + (tq + 1) * 512],
                                              start=(ec == 0), stop=(ec == EC - 1))
                                if ec == EC - 1:
                                    _inc(i_, s_mg)
                            for ec in range(EC):
                                i_ = t.matmul(pu[par][:, :],
                                              wuT[(fc // 2) % 2][:, ec * 256 + fo:ec * 256 + fo + 128],
                                              xnT[:, ec * T + tq * 512:ec * T + (tq + 1) * 512],
                                              start=(ec == 0), stop=(ec == EC - 1))
                                if ec == EC - 1:
                                    _inc(i_, s_mu)

                @blk1.scalar
                def _(sc):
                    for fc in range(NF if (p1mask & 4) else 0):
                        for tq in range(NQ1):
                            gi = fc * NQ1 + tq
                            par = gi % 2
                            sc.wait_ge(s_mg.h, s_mg.exp(gi + 1))
                            if gi >= 2 and (p1mask & 8):
                                sc.wait_ge(s_ml.h, s_ml.exp(gi - 1))
                            _inc(sc.activation(sg[par][:, :], pg[par][:, :], AF.Silu), s_si)

                @blk1.vector
                def _(v):
                    for fc in range(NF if (p1mask & 8) else 0):
                        for tq in range(NQ1):
                            gi = fc * NQ1 + tq
                            par = gi % 2
                            v.wait_ge(s_si.h, s_si.exp(gi + 1))
                            v.wait_ge(s_mu.h, s_mu.exp(gi + 1))
                            if fc >= 2 and tq == 0:
                                v.wait_ge(s_pt.h, s_pt.exp(fc - 1))
                            _inc(v.tensor_tensor(ptt[fc % 2][:, tq * 512:(tq + 1) * 512],
                                                 sg[par][:, :], pu[par][:, :], ALU.mult), s_ml)

                @blk1.gpsimd
                def _(g):
                    ncg = FF // 256
                    ncd = EMB // 256
                    q4l = EC * T // 4
                    nsp = 0
                    ncast = 0

                    def _cast(c):
                        if c >= 1:
                            g.wait_ge(s_cg.h, s_cg.exp(c))
                            g.wait_ge(s_cu.h, s_cu.exp(c))
                        _inc(g.dma_start(wgb[c * 256:(c + 1) * 256, :],
                                         wg[c * 256:(c + 1) * 256, :]), s_cg)
                        _inc(g.dma_start(wub[c * 256:(c + 1) * 256, :],
                                         wu[c * 256:(c + 1) * 256, :]), s_cu)

                    for c in range(min(3, ncg)):
                        _cast(c)
                        ncast = c + 1
                    nwd = 0
                    for fc in range(NF if (p1mask & 8) else 0):
                        if fc % 2 == 0 and ncast < ncg:
                            _cast(ncast)
                            ncast += 1
                        if fc % 2 == 1 and fc >= 11 and nwd < ncd:
                            d = nwd
                            if d >= 1:
                                g.wait_ge(s_cd.h, s_cd.exp(d))
                            _inc(g.dma_start(wdb[d * 256:(d + 1) * 256, :],
                                             wd[d * 256:(d + 1) * 256, :]), s_cd)
                            nwd += 1
                        if fc in (2, 4, 6, 8):
                            qq = nsp
                            if qq >= 1:
                                g.wait_ge(s_xb.h, s_xb.exp(qq))
                            _inc(g.dma_start(xnb[:, qq * q4l:(qq + 1) * q4l],
                                             xnT[:, qq * q4l:(qq + 1) * q4l]), s_xb)
                            nsp += 1
                        g.wait_ge(s_ml.h, s_ml.exp(NQ1 * (fc + 1)))
                        if fc >= 1:
                            g.wait_ge(s_pt.h, s_pt.exp(fc))
                        src = ptt[fc % 2][:, :].rearrange("p (tg t) -> p tg t", t=128)
                        dst = bass.AP(pts.tensor, fc * (T * 128),
                                      [[128, 128], [128 * 128, T // 128], [1, 128]])
                        _inc(g.dma_start(dst, src), s_pt)
                    while ncast < ncg:
                        _cast(ncast)
                        ncast += 1
                    while nsp < 4:
                        qq = nsp
                        if qq >= 1:
                            g.wait_ge(s_xb.h, s_xb.exp(qq))
                        _inc(g.dma_start(xnb[:, qq * q4l:(qq + 1) * q4l],
                                         xnT[:, qq * q4l:(qq + 1) * q4l]), s_xb)
                        nsp += 1
                    for d in range(nwd, ncd):
                        if d >= 1:
                            g.wait_ge(s_cd.h, s_cd.exp(d))
                        _inc(g.dma_start(wdb[d * 256:(d + 1) * 256, :],
                                         wd[d * 256:(d + 1) * 256, :]), s_cd)

            nc.all_engine_barrier()

        if phases < 2:
            return nc
        if True:
            with ExitStack() as p23:
                wcol = [p23.enter_context(nc.sbuf_tensor(f"wcol{i}", [128, EC * 128], BF16)) for i in range(2)]
                qgd = [p23.enter_context(nc.sbuf_tensor(f"qgd{i}", [128, 512], BF16)) for i in range(2)]
                qgt = p23.enter_context(nc.sbuf_tensor("qgt", [128, 2 * J * NSUB * 128], BF16))
                vall = p23.enter_context(nc.sbuf_tensor("vall", [128, (J + 1) * TCH], BF16))
                xnt2 = [p23.enter_context(nc.sbuf_tensor(f"xnt2_{i}", [128, EC * 512], BF16)) for i in range(2)]
                bq64 = [p23.enter_context(nc.sbuf_tensor(f"bq64_{i}", [J + 1, 64 * 128], BF16)) for i in range(2)]
                bk64 = [p23.enter_context(nc.sbuf_tensor(f"bk64_{i}", [J + 1, 64 * 128], BF16)) for i in range(2)]
                nm16 = [p23.enter_context(nc.sbuf_tensor(f"nm16_{i}", [128, 16], BF16)) for i in range(2)]
                nmr = [p23.enter_context(nc.sbuf_tensor(f"nmr{i}", [16, 128], BF16)) for i in range(2)]
                expt = [p23.enter_context(nc.sbuf_tensor(f"expt{i}", [128, 512], BF16)) for i in range(4)]
                sray = [p23.enter_context(nc.sbuf_tensor(f"sray{i}", [128, 8], F32)) for i in range(2)]
                avb = [p23.enter_context(nc.sbuf_tensor(f"avb{i}", [128, 64 * H], F32)) for i in range(2)]
                pqk = [p23.enter_context(nc.psum_tensor(f"pqk{i}", [128, 512], F32)) for i in range(2)]
                pt3 = [p23.enter_context(nc.psum_tensor(f"pt3{i}", [128, 512], F32)) for i in range(2)]
                pav = [p23.enter_context(nc.psum_tensor(f"pav{i}", [128, 512], F32)) for i in range(2)]
                ptr2 = [p23.enter_context(nc.psum_tensor(f"ptr2{i}", [128, 512], BF16)) for i in range(2)]

                blk2 = p23.enter_context(nc.Block("p23"))

                def OCG(tc, oc):
                    return tc * OCQ + oc

                def QKG(tc, oc):
                    return tc * 2 * J + oc

                def GRG(tc, s, g):
                    return (tc * NSUB + s) * 8 + g

                def CHG(tc, s, g, c):
                    return GRG(tc, s, g) * 4 + c

                def HFG(tc, s, g, hh):
                    return GRG(tc, s, g) * 2 + hh

                def B64(tc, s, half64):
                    return (tc * NSUB + s) * 2 + half64

                wsc = {0: wqgb, 1: wkb, 2: wvb}

                @blk2.gpsimd
                def _(g):
                    _inc(g.memset(bq64[0][:, :], 1.0), s_ini)
                    _inc(g.memset(bq64[1][:, :], 1.0), s_ini)
                    _inc(g.memset(bk64[0][:, :], 1.0), s_ini)
                    _inc(g.memset(bk64[1][:, :], 1.0), s_ini)
                    _inc(g.memset(vall[:, J * TCH:(J + 1) * TCH], 1.0), s_ini)
                    g.wait_ge(s_pw.h, s_pw.exp(3 * EC))
                    g.wait_ge(s_xb.h, s_xb.exp(4))
                    for tc in range(NTCH):
                        if tc >= 2:
                            g.wait_ge(s_m2.h, s_m2.exp((tc - 1) * OCQ))
                        if tc >= 1:
                            g.wait_ge(s_x2.h, s_x2.exp(tc))
                        src_ = bass.AP(xnb.tensor, tc * 512,
                                       [[EC * T, 128], [T, EC], [1, 512]])
                        _inc(g.dma_start(xnt2[tc % 2][:, :], src_), s_x2)
                        for oc in range(OCQ):
                            O = OCG(tc, oc)
                            if O >= 2:
                                g.wait_ge(s_m2.h, s_m2.exp(O - 1))
                            if O >= 1:
                                g.wait_ge(s_wc.h, s_wc.exp(O))
                            j = oc % J
                            src = bass.AP(wsc[oc // J].tensor, j * EMB * 128,
                                          [[128, 128], [128 * 128, EC], [1, 128]])
                            _inc(g.dma_start(
                                wcol[O % 2][:, :].rearrange("p (ec d) -> p ec d", d=128),
                                src), s_wc)

                @blk2.tensor
                def _(t):
                    t.wait_ge(s_ini.h, 7)
                    for tc in range(NTCH):
                        for oc in range(OCQ + 1):
                            if oc < OCQ:
                                O = OCG(tc, oc)
                                t.wait_ge(s_wc.h, s_wc.exp(O + 1))
                                if oc == 0:
                                    t.wait_ge(s_x2.h, s_x2.exp(tc + 1))
                                if O >= 2:
                                    t.wait_ge(s_e2.h, s_e2.exp(O - 1))
                                if oc < 2 and tc >= 1:
                                    t.wait_ge(s_rm.h, s_rm.exp(tc * NSUB * 32))
                                for ec in range(EC):
                                    i_ = t.matmul(pqk[O % 2][:, :],
                                                  wcol[O % 2][:, ec * 128:(ec + 1) * 128],
                                                  xnt2[tc % 2][:, ec * 512:(ec + 1) * 512],
                                                  start=(ec == 0), stop=(ec == EC - 1))
                                    if ec == EC - 1:
                                        _inc(i_, s_m2)
                            ocp = oc - 1
                            if 0 <= ocp < 2 * J:
                                Q = QKG(tc, ocp)
                                t.wait_ge(s_e2.h, s_e2.exp(OCG(tc, ocp) + 1))
                                if Q >= 2:
                                    t.wait_ge(s_te.h, s_te.exp(NSUB * (Q - 1)))
                                if ocp == 0 and tc >= 1:
                                    t.wait_ge(s_ne.h, s_ne.exp(tc * NGT))
                                for sub in range(NSUB):
                                    _inc(t.transpose(ptr2[Q % 2][:, sub * 128:(sub + 1) * 128],
                                                     qgd[Q % 2][:, sub * 128:(sub + 1) * 128],
                                                     ident[:, :]), s_t2)
                        for s in range(NSUB):
                            for g in range(8 + 2):
                                if g < 8:
                                    G = GRG(tc, s, g)
                                    if g % 4 == 0:
                                        HS = (tc * NSUB + s) * 2 + g // 4
                                        t.wait_ge(s_bl.h, s_bl.exp((HS + 1) * 2 * J))
                                    for c in range(4):
                                        C = CHG(tc, s, g, c)
                                        if C >= 2:
                                            t.wait_ge(s_rm.h, s_rm.exp(C - 1))
                                        if s == 0 and g == 0 and c < 2:
                                            t.wait_ge(s_e2.h, s_e2.exp(OCG(tc, OCQ - 1) + 1))
                                        for i in range(4):
                                            tt = g * 16 + c * 4 + i
                                            hs2 = tt // 64
                                            to = tt % 64
                                            mm = t.matmul(
                                                pqk[C % 2][:, i * 128:(i + 1) * 128],
                                                bq64[hs2 % 2][0:J, to * 128:(to + 1) * 128],
                                                bk64[hs2 % 2][0:J, to * 128:(to + 1) * 128],
                                                start=True, stop=True)
                                            if i == 3:
                                                _inc(mm, s_m1)
                                gp = g - 1
                                if 0 <= gp < 8:
                                    G = GRG(tc, s, gp)
                                    t.wait_ge(s_rm.h, s_rm.exp(4 * (G + 1)))
                                    if G >= 1:
                                        t.wait_ge(s_ne.h, s_ne.exp(G))
                                    _inc(t.transpose(ptr2[0][0:16, 0:128], nm16[G % 2][:, :],
                                                     ident[:, :]), s_nt)
                                gq = g - 2
                                if 0 <= gq < 8:
                                    G = GRG(tc, s, gq)
                                    for c in range(4):
                                        C = CHG(tc, s, gq, c)
                                        t.wait_ge(s_ex.h, s_ex.exp(C + 1))
                                        hh = c // 2
                                        HF = HFG(tc, s, gq, hh)
                                        if c % 2 == 0 and HF >= 2:
                                            t.wait_ge(s_no.h, s_no.exp(HF - 1))
                                        for i in range(4):
                                            tt = gq * 16 + c * 4 + i
                                            ttc = s * 128 + tt
                                            k = (c % 2) * 4 + i
                                            rhs = bass.AP(vall, ttc,
                                                          [[(J + 1) * TCH, 128], [TCH, J + 1], [0, 4]])
                                            mm = t.matmul(pav[HF % 2][:, k * VC:(k + 1) * VC],
                                                          expt[C % 4][:, i * 128:(i + 1) * 128],
                                                          rhs, start=True, stop=True)
                                            if i == 3:
                                                _inc(mm, s_m3)
                                if 0 <= gp < 8:
                                    G = GRG(tc, s, gp)
                                    t.wait_ge(s_in.h, s_in.exp(G + 1))
                                    for c in range(4):
                                        C = CHG(tc, s, gp, c)
                                        if C >= 2:
                                            t.wait_ge(s_ex.h, s_ex.exp(C - 1))
                                        for i in range(4):
                                            tt = gp * 16 + c * 4 + i
                                            hs2 = tt // 64
                                            to = tt % 64
                                            mm = t.matmul(
                                                pt3[C % 2][:, i * 128:(i + 1) * 128],
                                                bk64[hs2 % 2][0:J + 1, to * 128:(to + 1) * 128],
                                                bq64[hs2 % 2][0:J + 1, to * 128:(to + 1) * 128],
                                                start=True, stop=True)
                                            if i == 3:
                                                _inc(mm, s_mT)

                @blk2.vector
                def _(v):
                    for tc in range(NTCH):
                        for oc in range(OCQ):
                            O = OCG(tc, oc)
                            j = oc % J
                            v.wait_ge(s_m2.h, s_m2.exp(O + 1))
                            if oc < 2 * J:
                                Q = QKG(tc, oc)
                                if Q >= 2:
                                    v.wait_ge(s_t2.h, s_t2.exp(NSUB * (Q - 1)))
                                bias = bqg if oc < J else bkc
                                _inc(v.tensor_scalar(qgd[Q % 2][:, :], pqk[O % 2][:, :],
                                                     bias[:, j:j + 1], None, ALU.add), s_e2)
                            else:
                                if oc == 2 * J and tc >= 1:
                                    v.wait_ge(s_m3.h, s_m3.exp(tc * NSUB * 32))
                                _inc(v.tensor_scalar(vall[:, j * TCH:(j + 1) * TCH],
                                                     pqk[O % 2][:, :],
                                                     bvc[:, j:j + 1], None, ALU.add), s_e2)
                        for s in range(NSUB):
                            for g in range(8 + 2):
                                if g < 8:
                                    G = GRG(tc, s, g)
                                    for c in range(4):
                                        C = CHG(tc, s, g, c)
                                        v.wait_ge(s_m1.h, s_m1.exp(C + 1))
                                        if c == 0 and G >= 2:
                                            v.wait_ge(s_nt.h, s_nt.exp(G - 1))
                                        _inc(v.tensor_reduce(
                                            nm16[G % 2][:, c * 4:(c + 1) * 4],
                                            pqk[C % 2][:, :].rearrange("p (i e) -> p i e", i=4),
                                            AX.X, ALU.max, negate=True), s_rm)
                                gq = g - 2
                                if 0 <= gq < 8:
                                    G = GRG(tc, s, gq)
                                    for hh in range(2):
                                        HF = HFG(tc, s, gq, hh)
                                        v.wait_ge(s_m3.h, s_m3.exp(4 * G + 2 * hh + 2))
                                        rsum = bass.AP(pav[HF % 2], 4 * J,
                                                       [[512, 128], [VC, 8]])
                                        _inc(v.reciprocal(sray[HF % 2][:, :], rsum), s_rc)
                                        v.wait_ge(s_rc.h, s_rc.exp(HF + 1))
                                        hid = (gq * 2 + hh) % 8
                                        half64 = (gq * 2 + hh) // 8
                                        B = B64(tc, s, half64)
                                        if B >= 2 and hid == 0:
                                            v.wait_ge(s_as.h, s_as.exp(B - 1))
                                        for k in range(8):
                                            tok64 = hid * 8 + k
                                            ts_ = v.tensor_scalar(
                                                avb[B % 2][:, tok64 * H:(tok64 + 1) * H],
                                                pav[HF % 2][:, k * VC:k * VC + H],
                                                sray[HF % 2][:, k:k + 1], None, ALU.mult)
                                            if k == 7:
                                                _inc(ts_, s_no)

                @blk2.scalar
                def _(sc):
                    for tc in range(NTCH):
                        for ocp in range(2 * J):
                            Q = QKG(tc, ocp)
                            if ocp == 0 and tc >= 1:
                                sc.wait_ge(s_bl.h, s_bl.exp(tc * NSUB * 2 * J))
                            for sub in range(NSUB):
                                sc.wait_ge(s_t2.h, s_t2.exp(NSUB * (Q + 1)))
                                _inc(sc.copy(
                                    qgt[:, (ocp * NSUB + sub) * 128:(ocp * NSUB + sub + 1) * 128],
                                    ptr2[Q % 2][:, sub * 128:(sub + 1) * 128]), s_te)
                        for s in range(NSUB):
                            for g in range(8 + 2):
                                gp = g - 1
                                if 0 <= gp < 8:
                                    G = GRG(tc, s, gp)
                                    sc.wait_ge(s_nt.h, s_nt.exp(G + 1))
                                    if G >= 2:
                                        sc.wait_ge(s_in.h, s_in.exp(G - 1))
                                    _inc(sc.copy(nmr[G % 2][:, :], ptr2[0][0:16, 0:128]), s_ne)
                                    for c in range(4):
                                        C = CHG(tc, s, gp, c)
                                        sc.wait_ge(s_mT.h, s_mT.exp(C + 1))
                                        if C >= 4:
                                            sc.wait_ge(s_m3.h, s_m3.exp(C - 3))
                                        _inc(sc.activation(expt[C % 4][:, :], pt3[C % 2][:, :],
                                                           AF.Exp), s_ex)
                            for half64 in range(2):
                                B = B64(tc, s, half64)
                                sc.wait_ge(s_no.h, s_no.exp((B + 1) * 8))
                                if B >= 1:
                                    sc.wait_ge(s_as.h, s_as.exp(B))
                                base = tc * TCH + s * 128 + half64 * 64
                                dst = bass.AP(attn_s.tensor, base * EMB,
                                              [[H, 128], [EMB, 64], [1, H]])
                                _inc(sc.dma_start(dst, avb[B % 2][:, :].rearrange(
                                    "p (t h) -> p t h", h=H)), s_as)

                @blk2.sync
                def _(s):
                    for tc in range(NTCH):
                        for sub in range(NSUB):
                            SB = tc * NSUB + sub
                            for hs in range(2):
                                HS = SB * 2 + hs
                                s.wait_ge(s_te.h, s_te.exp((tc * 2 * J + 2 * J) * NSUB))
                                if HS >= 1:
                                    s.wait_ge(s_bl.h, s_bl.exp(HS * 2 * J))
                                if HS >= 2:
                                    s.wait_ge(s_mT.h, s_mT.exp((HS - 1) * 16))
                                for j in range(J):
                                    src = qgt[hs * 64:hs * 64 + 64,
                                              (j * NSUB + sub) * 128:(j * NSUB + sub + 1) * 128]
                                    dst = bass.AP(bq64[hs % 2], j * 64 * 128,
                                                  [[64 * 128, 1], [128, 64], [1, 128]])
                                    _inc(s.dma_start(dst, src), s_bl)
                                for j in range(J):
                                    src = qgt[hs * 64:hs * 64 + 64,
                                              ((J + j) * NSUB + sub) * 128:((J + j) * NSUB + sub + 1) * 128]
                                    dst = bass.AP(bk64[hs % 2], j * 64 * 128,
                                                  [[64 * 128, 1], [128, 64], [1, 128]])
                                    _inc(s.dma_start(dst, src), s_bl)
                            for g in range(8):
                                G = GRG(tc, sub, g)
                                s.wait_ge(s_ne.h, s_ne.exp(G + 1))
                                if G >= 1:
                                    s.wait_ge(s_in.h, s_in.exp(G))
                                dst = bass.AP(bq64[(g // 4) % 2],
                                              J * 64 * 128 + (g % 4) * 16 * 128,
                                              [[64 * 128, 1], [128, 16], [1, 128]])
                                _inc(s.dma_start(dst, nmr[G % 2][:, :]), s_in)

            nc.all_engine_barrier()

        if phases < 3:
            return nc
        with ExitStack() as p4:
            wdT = p4.enter_context(nc.sbuf_tensor("wdT", [128, NF * EP], BF16))
            ptsb = [p4.enter_context(nc.sbuf_tensor(f"ptsb{i}", [128, NF * 128], BF16)) for i in range(2)]
            xsl = [p4.enter_context(nc.sbuf_tensor(f"xsl{i}", [128, EP], F32)) for i in range(2)]
            asl = [p4.enter_context(nc.sbuf_tensor(f"asl{i}", [128, EP], F32)) for i in range(2)]
            osb = [p4.enter_context(nc.sbuf_tensor(f"osb{i}", [128, EP], F32)) for i in range(2)]
            py = [p4.enter_context(nc.psum_tensor(f"py{i}", [128, EP], F32)) for i in range(2)]

            blk4 = p4.enter_context(nc.Block("p4"))

            @blk4.sync
            def _(s):
                s.wait_ge(s_cd.h, s_cd.exp(EMB // 256))
                WB = 8
                for p in range(NEP):
                    if p >= 1:
                        s.wait_ge(s_my.h, s_my.exp(p * NTG))
                    for fc in range(NF):
                        if fc % WB == 0 and p * NF + fc > 0:
                            s.wait_ge(s_wt.h, s_wt.exp(p * NF + fc))
                        _inc(s.dma_start_transpose(
                            wdT[:, fc * EP:(fc + 1) * EP],
                            wdb[p * EP:(p + 1) * EP, fc * 128:(fc + 1) * 128]), s_wt)
                    for tg in range(NTG):
                        PTG = p * NTG + tg
                        s.wait_ge(s_cb.h, s_cb.exp(2 * (PTG + 1)))
                        if PTG >= 1:
                            s.wait_ge(s_ou.h, s_ou.exp(PTG))
                        _inc(s.dma_start(out[tg * 128:(tg + 1) * 128, p * EP:(p + 1) * EP],
                                         osb[PTG % 2][:, :]), s_ou)
                s.wait_ge(s_ou.h, s_ou.exp(NEP * NTG))

            @blk4.gpsimd
            def _(g):
                g.wait_ge(s_pt.h, s_pt.exp(NF))
                NH = NF // 2
                for p in range(NEP):
                    for tg in range(NTG):
                        PTG = p * NTG + tg
                        if PTG >= 2:
                            g.wait_ge(s_my.h, s_my.exp(PTG - 1))
                        if PTG >= 1:
                            g.wait_ge(s_p4.h, s_p4.exp(2 * PTG))
                        for hf in range(2):
                            src = bass.AP(pts.tensor, hf * NH * T * 128 + tg * 128 * 128,
                                          [[128, 128], [T * 128, NH], [1, 128]])
                            dst = ptsb[PTG % 2][:, hf * NH * 128:(hf + 1) * NH * 128].rearrange(
                                "p (f t) -> p f t", t=128)
                            _inc(g.dma_start(dst, src), s_p4)

            @blk4.scalar
            def _(sc):
                sc.wait_ge(s_as.h, s_as.exp(T // 64))
                for p in range(NEP):
                    for tg in range(NTG):
                        PTG = p * NTG + tg
                        if PTG >= 2:
                            sc.wait_ge(s_cb.h, s_cb.exp(2 * (PTG - 1)))
                        if PTG >= 1:
                            sc.wait_ge(s_xa.h, s_xa.exp(2 * PTG))
                        _inc(sc.dma_start(xsl[PTG % 2][:, :],
                                          x_in[tg * 128:(tg + 1) * 128, p * EP:(p + 1) * EP]), s_xa)
                        _inc(sc.dma_start(asl[PTG % 2][:, :],
                                          attn_s[tg * 128:(tg + 1) * 128, p * EP:(p + 1) * EP]), s_xa)

            @blk4.tensor
            def _(t):
                for p in range(NEP):
                    for tg in range(NTG):
                        PTG = p * NTG + tg
                        t.wait_ge(s_p4.h, s_p4.exp(2 * (PTG + 1)))
                        if PTG >= 2:
                            t.wait_ge(s_cb.h, s_cb.exp(2 * (PTG - 1)))
                        for fc in range(NF):
                            if tg == 0 and fc % 8 == 0:
                                t.wait_ge(s_wt.h, s_wt.exp(p * NF + min(NF, fc + 8)))
                            i_ = t.matmul(py[PTG % 2][:, :],
                                          ptsb[PTG % 2][:, fc * 128:(fc + 1) * 128],
                                          wdT[:, fc * EP:(fc + 1) * EP],
                                          start=(fc == 0), stop=(fc == NF - 1))
                            if fc == NF - 1:
                                _inc(i_, s_my)

            @blk4.vector
            def _(v):
                for p in range(NEP):
                    for tg in range(NTG):
                        PTG = p * NTG + tg
                        v.wait_ge(s_my.h, s_my.exp(PTG + 1))
                        v.wait_ge(s_xa.h, s_xa.exp(2 * (PTG + 1)))
                        if PTG >= 2:
                            v.wait_ge(s_ou.h, s_ou.exp(PTG - 1))
                        _inc(v.tensor_add(osb[PTG % 2][:, :], xsl[PTG % 2][:, :],
                                          py[PTG % 2][:, :]), s_cb)
                        v.wait_ge(s_cb.h, s_cb.exp(2 * PTG + 1))
                        _inc(v.tensor_add(osb[PTG % 2][:, :], osb[PTG % 2][:, :],
                                          asl[PTG % 2][:, :]), s_cb)

    return nc


_PROG_CACHE = {}


def _get_prog(T, EMB, FF):
    key = (T, EMB, FF)
    if key not in _PROG_CACHE:
        _PROG_CACHE[key] = build_program(T, EMB, FF)
    return _PROG_CACHE[key]


def kernel(L0_attn_x, L0_attn_Wq, L0_attn_bq, L0_attn_Wk, L0_attn_bk,
           L0_attn_Wv, L0_attn_bv, L0_ffn_W_gate, L0_ffn_W_up, L0_ffn_W_down):
    x = np.ascontiguousarray(np.asarray(L0_attn_x, dtype=np.float32))
    B, S, EMB = x.shape
    NTOK = B * S
    T = NTOK // N_CORES
    x2 = x.reshape(NTOK, EMB)
    wq2 = np.ascontiguousarray(np.asarray(L0_attn_Wq, dtype=np.float32).reshape(EMB, -1))
    wk2 = np.ascontiguousarray(np.asarray(L0_attn_Wk, dtype=np.float32).reshape(EMB, -1))
    wv2 = np.ascontiguousarray(np.asarray(L0_attn_Wv, dtype=np.float32).reshape(EMB, -1))
    bq2 = np.ascontiguousarray(np.asarray(L0_attn_bq, dtype=np.float32))
    bk2 = np.ascontiguousarray(np.asarray(L0_attn_bk, dtype=np.float32))
    bv2 = np.ascontiguousarray(np.asarray(L0_attn_bv, dtype=np.float32))
    wg2 = np.ascontiguousarray(np.asarray(L0_ffn_W_gate, dtype=np.float32))
    wu2 = np.ascontiguousarray(np.asarray(L0_ffn_W_up, dtype=np.float32))
    wd2 = np.ascontiguousarray(np.asarray(L0_ffn_W_down, dtype=np.float32))
    FF = wg2.shape[0]

    nc = _get_prog(T, EMB, FF)
    in_maps = []
    for c in range(N_CORES):
        in_maps.append({
            "x": x2[c * T:(c + 1) * T],
            "wq": wq2, "wk": wk2, "wv": wv2,
            "bq": bq2, "bk": bk2, "bv": bv2,
            "wg": wg2, "wu": wu2, "wd": wd2,
        })
    res = run_bass_kernel_spmd(nc, in_maps, core_ids=list(range(N_CORES)))
    outs = [np.asarray(res.results[c]["out"]) for c in range(N_CORES)]
    return np.concatenate(outs, axis=0).reshape(B, S, EMB).astype(np.float32)


# version 09
# speedup vs baseline: 1.0667x; 1.0071x over previous
import numpy as np
import ml_dtypes
from contextlib import ExitStack

import concourse.bass as bass
import concourse.mybir as mybir
from concourse.bass_utils import run_bass_kernel_spmd

F32 = mybir.dt.float32
BF16 = mybir.dt.bfloat16
AF = mybir.ActivationFunctionType
ALU = mybir.AluOpType
AX = mybir.AxisListType

N_CORES = 8


class SemT:
    def __init__(self, handle, unit):
        self.h = handle
        self.unit = unit

    def exp(self, k):
        return k * self.unit


def _inc(inst, sem):
    inst.then_inc(sem.h, sem.unit)
    return inst


def build_program(T=2048, EMB=4096, FF=11008, detect_races=False, phases=4, p0mask=15, p1mask=15):
    assert EMB % 512 == 0 and T % 512 == 0 and FF % 256 == 0
    EC = EMB // 128
    H = EMB // 128
    J = H // 4
    NF = FF // 128
    TCH = 512
    NTCH = T // TCH
    NSUB = TCH // 128
    NQ1 = T // 512
    NT0 = T // 128
    NTG = T // 128
    EP = 512
    NEP = EMB // EP
    NGT = NSUB * 8
    VC = 4 * (J + 1)
    SCALE = 1.0 / float(np.sqrt(128.0))
    OCQ = 3 * J
    PVC = 3 + 3 * J

    nc = bass.Bass("TRN2", target_bir_lowering=False,
                   detect_race_conditions=detect_races)

    x_in = nc.dram_tensor("x", [T, EMB], F32, kind="ExternalInput").ap()
    wq = nc.dram_tensor("wq", [EMB, EMB], F32, kind="ExternalInput").ap()
    wk = nc.dram_tensor("wk", [EMB, 128 * J], F32, kind="ExternalInput").ap()
    wv = nc.dram_tensor("wv", [EMB, 128 * J], F32, kind="ExternalInput").ap()
    bq = nc.dram_tensor("bq", [128, H], F32, kind="ExternalInput").ap()
    bk = nc.dram_tensor("bk", [128, J], F32, kind="ExternalInput").ap()
    bv = nc.dram_tensor("bv", [128, J], F32, kind="ExternalInput").ap()
    wg = nc.dram_tensor("wg", [FF, EMB], F32, kind="ExternalInput").ap()
    wu = nc.dram_tensor("wu", [FF, EMB], F32, kind="ExternalInput").ap()
    wd = nc.dram_tensor("wd", [EMB, FF], F32, kind="ExternalInput").ap()
    out = nc.dram_tensor("out", [T, EMB], F32, kind="ExternalOutput").ap()

    wgb = nc.dram_tensor("wgb", [FF, EMB], BF16, kind="Internal").ap()
    wub = nc.dram_tensor("wub", [FF, EMB], BF16, kind="Internal").ap()
    wdb = nc.dram_tensor("wdb", [EMB, FF], BF16, kind="Internal").ap()
    wqgb = nc.dram_tensor("wqgb", [J * EMB * 128], BF16, kind="Internal").ap()
    wkb = nc.dram_tensor("wkb", [J * EMB * 128], BF16, kind="Internal").ap()
    wvb = nc.dram_tensor("wvb", [J * EMB * 128], BF16, kind="Internal").ap()
    pts = nc.dram_tensor("pts", [NF * T * 128], BF16, kind="Internal").ap()
    xnb = nc.dram_tensor("xnb", [128, EC * T], BF16, kind="Internal").ap()
    attn_s = nc.dram_tensor("attn_s", [T, EMB], F32, kind="Internal").ap()

    with ExitStack() as root:
        def SEM(name, unit=1):
            return SemT(root.enter_context(nc.semaphore(name)), unit)

        s_x = SEM("s_x", 16)
        s_ln = SEM("s_ln")
        s_sc0 = SEM("s_sc0")
        s_xt = SEM("s_xt")
        s_xe = SEM("s_xe")
        s_cg = SEM("s_cg", 16)
        s_cu = SEM("s_cu", 16)
        s_cd = SEM("s_cd", 16)
        s_pq = SEM("s_pq", 16)
        s_pk = SEM("s_pk", 16)
        s_sk = SEM("s_sk", 16)
        s_sv = SEM("s_sv", 16)
        s_pv = SEM("s_pv")
        s_pw = SEM("s_pw", 16)
        s_ini = SEM("s_ini")
        s_tr = SEM("s_tr", 16)
        s_mg = SEM("s_mg")
        s_mu = SEM("s_mu")
        s_mf = SEM("s_mf")
        s_si = SEM("s_si")
        s_ml = SEM("s_ml")
        s_pt = SEM("s_pt", 16)
        s_xb = SEM("s_xb", 16)
        s_x2 = SEM("s_x2", 16)
        s_wc = SEM("s_wc", 16)
        s_m2 = SEM("s_m2")
        s_e2 = SEM("s_e2")
        s_t2 = SEM("s_t2")
        s_te = SEM("s_te")
        s_bl = SEM("s_bl", 16)
        s_m1 = SEM("s_m1")
        s_rm = SEM("s_rm")
        s_nt = SEM("s_nt")
        s_ne = SEM("s_ne")
        s_in = SEM("s_in", 16)
        s_mT = SEM("s_mT")
        s_ex = SEM("s_ex")
        s_m3 = SEM("s_m3")
        s_rc = SEM("s_rc")
        s_no = SEM("s_no")
        s_as = SEM("s_as", 16)
        s_wt = SEM("s_wt", 16)
        s_p4 = SEM("s_p4", 16)
        s_xa = SEM("s_xa", 16)
        s_my = SEM("s_my")
        s_cb = SEM("s_cb")
        s_ou = SEM("s_ou", 16)

        ident = root.enter_context(nc.sbuf_tensor("ident", [128, 128], BF16))
        bqg = root.enter_context(nc.sbuf_tensor("bqg", [128, J], F32))
        bkc = root.enter_context(nc.sbuf_tensor("bkc", [128, J], F32))
        bvc = root.enter_context(nc.sbuf_tensor("bvc", [128, J], F32))
        bqt = root.enter_context(nc.sbuf_tensor("bqt", [128, H], F32))
        bq2 = root.enter_context(nc.sbuf_tensor("bq2", [128, J], F32))

        with ExitStack() as scope_x:
            xnT = scope_x.enter_context(nc.sbuf_tensor("xnT", [128, EC * T], BF16))

            with ExitStack() as p0:
                xt = p0.enter_context(nc.sbuf_tensor("xt", [128, EMB], F32))
                xsq = p0.enter_context(nc.sbuf_tensor("xsq", [128, EMB], F32))
                xn = p0.enter_context(nc.sbuf_tensor("xn", [128, EMB], BF16))
                stt = [p0.enter_context(nc.sbuf_tensor(f"stt{i}", [128, 8], F32)) for i in range(2)]
                pqf = p0.enter_context(nc.sbuf_tensor("pqf", [128, EMB], F32))
                prd = p0.enter_context(nc.sbuf_tensor("prd", [128, EMB // 4], F32))
                pjq = p0.enter_context(nc.sbuf_tensor("pjq", [128, 128 * J], BF16))
                skb = p0.enter_context(nc.sbuf_tensor("skb", [128, 128 * J], BF16))
                svb = p0.enter_context(nc.sbuf_tensor("svb", [128, 128 * J], BF16))
                pjk = p0.enter_context(nc.sbuf_tensor("pjk", [128, 128 * J], BF16))
                pjv = p0.enter_context(nc.sbuf_tensor("pjv", [128, 128 * J], BF16))
                ptx = [p0.enter_context(nc.psum_tensor(f"ptx{i}", [128, 512], BF16)) for i in range(2)]

                blk0 = p0.enter_context(nc.Block("p0"))

                @blk0.sync
                def _(s):
                    for i in range(NT0 if (p0mask & 1) else 0):
                        if i >= 1:
                            s.wait_ge(s_sc0.h, s_sc0.exp(2 * (i - 1) + 1))
                            s.wait_ge(s_ln.h, s_ln.exp(7 * (i - 1) + 7))
                        _inc(s.dma_start(xt[:, :], x_in[i * 128:(i + 1) * 128, :]), s_x)

                @blk0.scalar
                def _(sc):
                    for i in range(NT0 if (p0mask & 1) else 0):
                        p = i % 2
                        st = stt[p]
                        sc.wait_ge(s_x.h, s_x.exp(i + 1))
                        if i >= 2:
                            sc.wait_ge(s_ln.h, s_ln.exp(7 * (i - 2) + 7))
                        _inc(sc.activation(xsq[:, :], xt[:, :], AF.Square,
                                           accum_out=st[:, 1:2]), s_sc0)
                        sc.wait_ge(s_ln.h, s_ln.exp(7 * i + 5))
                        _inc(sc.activation(st[:, 6:7], st[:, 5:6], AF.Sqrt), s_sc0)

                @blk0.vector
                def _(v):
                    if not (p0mask & 8):
                        return
                    v.wait_ge(s_pk.h, s_pk.exp(3))
                    q4 = bqt[:, :].rearrange("p (h r) -> p h r", r=4)
                    _inc(v.tensor_add(bq2[:, :], q4[:, :, 0], q4[:, :, 1]), s_pv)
                    _inc(v.tensor_add(bqg[:, :], q4[:, :, 2], q4[:, :, 3]), s_pv)
                    v.wait_ge(s_pv.h, s_pv.exp(2))
                    _inc(v.tensor_add(bqg[:, :], bqg[:, :], bq2[:, :]), s_pv)
                    v.wait_ge(s_pv.h, s_pv.exp(3))
                    _inc(v.tensor_scalar_mul(bqg[:, :], bqg[:, :], SCALE), s_pv)
                    def _prep_chain(ec):
                        base = 4 + PVC * ec
                        v.wait_ge(s_pq.h, s_pq.exp(ec + 1))
                        if ec >= 1:
                            v.wait_ge(s_pw.h, s_pw.exp(3 * ec))
                        q4 = pqf[:, :].rearrange("p (c r) -> p c r", r=4)
                        _inc(v.tensor_add(prd[:, :], q4[:, :, 0], q4[:, :, 1]), s_pv)
                        v.wait_ge(s_pv.h, s_pv.exp(base + 1))
                        _inc(v.tensor_add(prd[:, :], prd[:, :], q4[:, :, 2]), s_pv)
                        v.wait_ge(s_pv.h, s_pv.exp(base + 2))
                        _inc(v.tensor_add(prd[:, :], prd[:, :], q4[:, :, 3]), s_pv)
                        v.wait_ge(s_pv.h, s_pv.exp(base + 3))
                        rj = prd[:, :].rearrange("p (d j) -> p d j", j=J)
                        for j in range(J):
                            _inc(v.tensor_scalar_mul(pjq[:, j * 128:(j + 1) * 128],
                                                     rj[:, :, j], SCALE), s_pv)
                        v.wait_ge(s_sk.h, s_sk.exp(ec + 1))
                        if ec >= 1:
                            v.wait_ge(s_pw.h, s_pw.exp(3 * ec))
                        rk = skb[:, :].rearrange("p (e j) -> p e j", j=J)
                        for j in range(J):
                            _inc(v.tensor_copy(pjk[:, j * 128:(j + 1) * 128], rk[:, :, j]), s_pv)
                        v.wait_ge(s_sv.h, s_sv.exp(ec + 1))
                        if ec >= 1:
                            v.wait_ge(s_pw.h, s_pw.exp(3 * ec))
                        rv = svb[:, :].rearrange("p (e j) -> p e j", j=J)
                        for j in range(J):
                            _inc(v.tensor_copy(pjv[:, j * 128:(j + 1) * 128], rv[:, :, j]), s_pv)
                    for i in range(NT0 if (p0mask & 1) else 0):
                        p = i % 2
                        st = stt[p]
                        v.wait_ge(s_x.h, s_x.exp(i + 1))
                        if i >= 2:
                            v.wait_ge(s_sc0.h, s_sc0.exp(2 * (i - 2) + 2))
                        _inc(v.reduce_sum(st[:, 0:1], xt[:, :], AX.X), s_ln)
                        v.wait_ge(s_sc0.h, s_sc0.exp(2 * i + 1))
                        v.wait_ge(s_ln.h, s_ln.exp(7 * i + 1))
                        _inc(v.tensor_scalar_mul(st[:, 2:3], st[:, 0:1], 1.0 / EMB), s_ln)
                        _inc(v.tensor_scalar_mul(st[:, 3:4], st[:, 1:2], 1.0 / EMB), s_ln)
                        v.wait_ge(s_ln.h, s_ln.exp(7 * i + 3))
                        _inc(v.tensor_mul(st[:, 4:5], st[:, 2:3], st[:, 2:3]), s_ln)
                        v.wait_ge(s_ln.h, s_ln.exp(7 * i + 4))
                        _inc(v.tensor_scalar(st[:, 5:6], st[:, 3:4], st[:, 4:5], 1e-5,
                                             ALU.subtract, ALU.add), s_ln)
                        v.wait_ge(s_sc0.h, s_sc0.exp(2 * i + 2))
                        _inc(v.reciprocal(st[:, 6:7], st[:, 6:7]), s_ln)
                        if i >= 1 and not (p0mask & 16):
                            for ec in range(EC):
                                gidx = (i - 1) * EC + ec
                                bank = (gidx // 4) % 2
                                slot = gidx % 4
                                v.wait_ge(s_xt.h, s_xt.exp(4 * (gidx // 4 + 1)))
                                _inc(v.tensor_copy(
                                    xnT[:, ec * T + (i - 1) * 128:ec * T + i * 128],
                                    ptx[bank][:, slot * 128:(slot + 1) * 128]), s_xe)
                        v.wait_ge(s_ln.h, s_ln.exp(7 * i + 6))
                        if i >= 1 and not (p0mask & 16):
                            v.wait_ge(s_xt.h, s_xt.exp(EC * i))
                        _inc(v.tensor_scalar(xn[:, :], xt[:, :], st[:, 2:3], st[:, 6:7],
                                             ALU.subtract, ALU.mult), s_ln)
                        if p0mask & 4:
                            for ec in (2 * i, 2 * i + 1):
                                if ec < EC:
                                    _prep_chain(ec)
                    for ec in range(EC if (p0mask & 1) and not (p0mask & 16) else 0):
                        gidx = (NT0 - 1) * EC + ec
                        bank = (gidx // 4) % 2
                        slot = gidx % 4
                        v.wait_ge(s_xt.h, s_xt.exp(4 * (gidx // 4 + 1)))
                        _inc(v.tensor_copy(
                            xnT[:, ec * T + (NT0 - 1) * 128:ec * T + NT0 * 128],
                            ptx[bank][:, slot * 128:(slot + 1) * 128]), s_xe)
                    if p0mask & 4:
                        for ec in range(2 * NT0, EC):
                            _prep_chain(ec)

                @blk0.gpsimd
                def _(g):
                    _inc(g.memset(ident[:, :], 0.0), s_ini)
                    g.wait_ge(s_ini.h, 1)
                    _inc(g.affine_select(out=ident[:, :], in_=ident[:, :],
                                         compare_op=ALU.not_equal, fill=1.0, base=0,
                                         pattern=[[-1, 128]], channel_multiplier=1), s_ini)
                    _inc(g.dma_start(bqt[:, :], bq[:, :]), s_pk)
                    _inc(g.dma_start(bkc[:, :], bk[:, :]), s_pk)
                    _inc(g.dma_start(bvc[:, :], bv[:, :]), s_pk)
                    def _emit_writes(ec):
                        base = 4 + PVC * ec
                        if ec >= 1:
                            g.wait_ge(s_pw.h, s_pw.exp(3 * ec))
                        g.wait_ge(s_pv.h, s_pv.exp(base + 3 + J))
                        dst = bass.AP(wqgb.tensor, ec * 128 * 128,
                                      [[128, 128], [EMB * 128, J], [1, 128]])
                        _inc(g.dma_start(dst, pjq[:, :].rearrange("p (j d) -> p j d", d=128)), s_pw)
                        g.wait_ge(s_pv.h, s_pv.exp(base + 3 + 2 * J))
                        dst = bass.AP(wkb.tensor, ec * 128 * 128,
                                      [[128, 128], [EMB * 128, J], [1, 128]])
                        _inc(g.dma_start(dst, pjk[:, :].rearrange("p (j d) -> p j d", d=128)), s_pw)
                        g.wait_ge(s_pv.h, s_pv.exp(base + 3 + 3 * J))
                        dst = bass.AP(wvb.tensor, ec * 128 * 128,
                                      [[128, 128], [EMB * 128, J], [1, 128]])
                        _inc(g.dma_start(dst, pjv[:, :].rearrange("p (j d) -> p j d", d=128)), s_pw)
                    for ec in range(EC if (p0mask & 4) else 0):
                        if ec >= 1:
                            _emit_writes(ec - 1)
                            g.wait_ge(s_pv.h, s_pv.exp(4 + PVC * ec))
                        _inc(g.dma_start(pqf[:, :], wq[ec * 128:(ec + 1) * 128, :]), s_pq)
                        _inc(g.dma_start(skb[:, :], wk[ec * 128:(ec + 1) * 128, :]), s_sk)
                        _inc(g.dma_start(svb[:, :], wv[ec * 128:(ec + 1) * 128, :]), s_sv)
                    if p0mask & 4:
                        _emit_writes(EC - 1)

                @blk0.tensor
                def _(t):
                    t.wait_ge(s_ini.h, 2)
                    for i in range(NT0 if (p0mask & 1) and not (p0mask & 16) else 0):
                        t.wait_ge(s_ln.h, s_ln.exp(7 * i + 7))
                        for ec in range(EC):
                            gidx = i * EC + ec
                            grp = gidx // 4
                            bank = grp % 2
                            slot = gidx % 4
                            if slot == 0 and grp >= 2:
                                t.wait_ge(s_xe.h, s_xe.exp(4 * (grp - 1)))
                            _inc(t.transpose(ptx[bank][:, slot * 128:(slot + 1) * 128],
                                             xn[:, ec * 128:(ec + 1) * 128],
                                             ident[:, :]), s_xt)

            nc.all_engine_barrier()

            if phases < 1:
                return nc
            with ExitStack() as p1:
                wgT = [p1.enter_context(nc.sbuf_tensor(f"wgT{i}", [128, EC * 256], BF16)) for i in range(2)]
                wuT = [p1.enter_context(nc.sbuf_tensor(f"wuT{i}", [128, EC * 256], BF16)) for i in range(2)]
                sg = [p1.enter_context(nc.sbuf_tensor(f"sg{i}", [128, 512], F32)) for i in range(2)]
                ptt = [p1.enter_context(nc.sbuf_tensor(f"ptt{i}", [128, T], BF16)) for i in range(2)]
                pg = [p1.enter_context(nc.psum_tensor(f"pg{i}", [128, 512], F32)) for i in range(2)]
                pu = [p1.enter_context(nc.psum_tensor(f"pu{i}", [128, 512], F32)) for i in range(2)]

                blk1 = p1.enter_context(nc.Block("p1"))

                NP = NF // 2

                @blk1.sync
                def _(s):
                    for P in range(NP if (p1mask & 1) else 0):
                        s.wait_ge(s_cg.h, s_cg.exp(P + 1))
                        s.wait_ge(s_cu.h, s_cu.exp(P + 1))
                        if P >= 2 and (p1mask & 2):
                            s.wait_ge(s_mu.h, s_mu.exp(NQ1 * (2 * P - 2)))
                        for k in range(2 * EC):
                            if k % 8 == 0 and 2 * EC * P + k > 0:
                                s.wait_ge(s_tr.h, s_tr.exp(2 * EC * P + k))
                            if k < EC:
                                _inc(s.dma_start_transpose(
                                    wgT[P % 2][:, k * 256:(k + 1) * 256],
                                    wgb[P * 256:(P + 1) * 256, k * 128:(k + 1) * 128]), s_tr)
                            else:
                                ec = k - EC
                                _inc(s.dma_start_transpose(
                                    wuT[P % 2][:, ec * 256:(ec + 1) * 256],
                                    wub[P * 256:(P + 1) * 256, ec * 128:(ec + 1) * 128]), s_tr)

                @blk1.tensor
                def _(t):
                    for fc in range(NF if (p1mask & 2) else 0):
                        P = fc // 2
                        first = fc % 2 == 0
                        if not first:
                            t.wait_ge(s_tr.h, s_tr.exp(2 * EC * (P + 1)))
                        for tq in range(NQ1):
                            gi = fc * NQ1 + tq
                            par = gi % 2
                            if gi >= 2 and (p1mask & 4):
                                t.wait_ge(s_si.h, s_si.exp(gi - 1))
                            if gi >= 2 and (p1mask & 8):
                                t.wait_ge(s_ml.h, s_ml.exp(gi - 1))
                            fo = (fc % 2) * 128
                            for ec in range(EC):
                                if first and tq == 0 and ec % 8 == 0:
                                    t.wait_ge(s_tr.h, s_tr.exp(2 * EC * P + min(2 * EC, ec + 8)))
                                i_ = t.matmul(pg[par][:, :],
                                              wgT[(fc // 2) % 2][:, ec * 256 + fo:ec * 256 + fo + 128],
                                              xnT[:, ec * T + tq * 512:ec * T + (tq + 1) * 512],
                                              start=(ec == 0), stop=(ec == EC - 1))
                                if ec == EC - 1:
                                    _inc(i_, s_mg)
                            for ec in range(EC):
                                if first and tq == 0 and ec % 8 == 0:
                                    t.wait_ge(s_tr.h, s_tr.exp(2 * EC * P + min(2 * EC, EC + ec + 8)))
                                i_ = t.matmul(pu[par][:, :],
                                              wuT[(fc // 2) % 2][:, ec * 256 + fo:ec * 256 + fo + 128],
                                              xnT[:, ec * T + tq * 512:ec * T + (tq + 1) * 512],
                                              start=(ec == 0), stop=(ec == EC - 1))
                                if ec == EC - 1:
                                    _inc(i_, s_mu)

                @blk1.scalar
                def _(sc):
                    for fc in range(NF if (p1mask & 4) else 0):
                        for tq in range(NQ1):
                            gi = fc * NQ1 + tq
                            par = gi % 2
                            sc.wait_ge(s_mg.h, s_mg.exp(gi + 1))
                            if gi >= 2 and (p1mask & 8):
                                sc.wait_ge(s_ml.h, s_ml.exp(gi - 1))
                            _inc(sc.activation(sg[par][:, :], pg[par][:, :], AF.Silu), s_si)

                @blk1.vector
                def _(v):
                    for fc in range(NF if (p1mask & 8) else 0):
                        for tq in range(NQ1):
                            gi = fc * NQ1 + tq
                            par = gi % 2
                            v.wait_ge(s_si.h, s_si.exp(gi + 1))
                            v.wait_ge(s_mu.h, s_mu.exp(gi + 1))
                            if fc >= 2 and tq == 0:
                                v.wait_ge(s_pt.h, s_pt.exp(fc - 1))
                            _inc(v.tensor_tensor(ptt[fc % 2][:, tq * 512:(tq + 1) * 512],
                                                 sg[par][:, :], pu[par][:, :], ALU.mult), s_ml)

                @blk1.gpsimd
                def _(g):
                    ncg = FF // 256
                    ncd = EMB // 256
                    q4l = EC * T // 4
                    nsp = 0
                    ncast = 0

                    def _cast(c):
                        if c >= 1:
                            g.wait_ge(s_cg.h, s_cg.exp(c))
                            g.wait_ge(s_cu.h, s_cu.exp(c))
                        _inc(g.dma_start(wgb[c * 256:(c + 1) * 256, :],
                                         wg[c * 256:(c + 1) * 256, :]), s_cg)
                        _inc(g.dma_start(wub[c * 256:(c + 1) * 256, :],
                                         wu[c * 256:(c + 1) * 256, :]), s_cu)

                    for c in range(min(3, ncg)):
                        _cast(c)
                        ncast = c + 1
                    nwd = 0
                    for fc in range(NF if (p1mask & 8) else 0):
                        if fc % 2 == 0 and ncast < ncg:
                            _cast(ncast)
                            ncast += 1
                        if fc % 2 == 1 and fc >= 11 and nwd < ncd:
                            d = nwd
                            if d >= 1:
                                g.wait_ge(s_cd.h, s_cd.exp(d))
                            _inc(g.dma_start(wdb[d * 256:(d + 1) * 256, :],
                                             wd[d * 256:(d + 1) * 256, :]), s_cd)
                            nwd += 1
                        if fc in (2, 4, 6, 8):
                            qq = nsp
                            if qq >= 1:
                                g.wait_ge(s_xb.h, s_xb.exp(qq))
                            _inc(g.dma_start(xnb[:, qq * q4l:(qq + 1) * q4l],
                                             xnT[:, qq * q4l:(qq + 1) * q4l]), s_xb)
                            nsp += 1
                        g.wait_ge(s_ml.h, s_ml.exp(NQ1 * (fc + 1)))
                        if fc >= 1:
                            g.wait_ge(s_pt.h, s_pt.exp(fc))
                        src = ptt[fc % 2][:, :].rearrange("p (tg t) -> p tg t", t=128)
                        dst = bass.AP(pts.tensor, fc * (T * 128),
                                      [[128, 128], [128 * 128, T // 128], [1, 128]])
                        _inc(g.dma_start(dst, src), s_pt)
                    while ncast < ncg:
                        _cast(ncast)
                        ncast += 1
                    while nsp < 4:
                        qq = nsp
                        if qq >= 1:
                            g.wait_ge(s_xb.h, s_xb.exp(qq))
                        _inc(g.dma_start(xnb[:, qq * q4l:(qq + 1) * q4l],
                                         xnT[:, qq * q4l:(qq + 1) * q4l]), s_xb)
                        nsp += 1
                    for d in range(nwd, ncd):
                        if d >= 1:
                            g.wait_ge(s_cd.h, s_cd.exp(d))
                        _inc(g.dma_start(wdb[d * 256:(d + 1) * 256, :],
                                         wd[d * 256:(d + 1) * 256, :]), s_cd)

            nc.all_engine_barrier()

        if phases < 2:
            return nc
        if True:
            with ExitStack() as p23:
                wcol = [p23.enter_context(nc.sbuf_tensor(f"wcol{i}", [128, EC * 128], BF16)) for i in range(2)]
                qgd = [p23.enter_context(nc.sbuf_tensor(f"qgd{i}", [128, 512], BF16)) for i in range(2)]
                qgt = p23.enter_context(nc.sbuf_tensor("qgt", [128, 2 * J * NSUB * 128], BF16))
                vall = p23.enter_context(nc.sbuf_tensor("vall", [128, (J + 1) * TCH], BF16))
                xnt2 = [p23.enter_context(nc.sbuf_tensor(f"xnt2_{i}", [128, EC * 512], BF16)) for i in range(2)]
                bq64 = [p23.enter_context(nc.sbuf_tensor(f"bq64_{i}", [J + 1, 64 * 128], BF16)) for i in range(2)]
                bk64 = [p23.enter_context(nc.sbuf_tensor(f"bk64_{i}", [J + 1, 64 * 128], BF16)) for i in range(2)]
                nm16 = [p23.enter_context(nc.sbuf_tensor(f"nm16_{i}", [128, 16], BF16)) for i in range(2)]
                nmr = [p23.enter_context(nc.sbuf_tensor(f"nmr{i}", [16, 128], BF16)) for i in range(2)]
                expt = [p23.enter_context(nc.sbuf_tensor(f"expt{i}", [128, 512], BF16)) for i in range(4)]
                sray = [p23.enter_context(nc.sbuf_tensor(f"sray{i}", [128, 8], F32)) for i in range(2)]
                avb = [p23.enter_context(nc.sbuf_tensor(f"avb{i}", [128, 64 * H], F32)) for i in range(2)]
                pqk = [p23.enter_context(nc.psum_tensor(f"pqk{i}", [128, 512], F32)) for i in range(2)]
                pt3 = [p23.enter_context(nc.psum_tensor(f"pt3{i}", [128, 512], F32)) for i in range(2)]
                pav = [p23.enter_context(nc.psum_tensor(f"pav{i}", [128, 512], F32)) for i in range(2)]
                ptr2 = [p23.enter_context(nc.psum_tensor(f"ptr2{i}", [128, 512], BF16)) for i in range(2)]

                blk2 = p23.enter_context(nc.Block("p23"))

                def OCG(tc, oc):
                    return tc * OCQ + oc

                def QKG(tc, oc):
                    return tc * 2 * J + oc

                def GRG(tc, s, g):
                    return (tc * NSUB + s) * 8 + g

                def CHG(tc, s, g, c):
                    return GRG(tc, s, g) * 4 + c

                def HFG(tc, s, g, hh):
                    return GRG(tc, s, g) * 2 + hh

                def B64(tc, s, half64):
                    return (tc * NSUB + s) * 2 + half64

                wsc = {0: wqgb, 1: wkb, 2: wvb}

                @blk2.gpsimd
                def _(g):
                    _inc(g.memset(bq64[0][:, :], 1.0), s_ini)
                    _inc(g.memset(bq64[1][:, :], 1.0), s_ini)
                    _inc(g.memset(bk64[0][:, :], 1.0), s_ini)
                    _inc(g.memset(bk64[1][:, :], 1.0), s_ini)
                    _inc(g.memset(vall[:, J * TCH:(J + 1) * TCH], 1.0), s_ini)
                    g.wait_ge(s_pw.h, s_pw.exp(3 * EC))
                    g.wait_ge(s_xb.h, s_xb.exp(4))
                    for tc in range(NTCH):
                        if tc >= 2:
                            g.wait_ge(s_m2.h, s_m2.exp((tc - 1) * OCQ))
                        if tc >= 1:
                            g.wait_ge(s_x2.h, s_x2.exp(tc))
                        src_ = bass.AP(xnb.tensor, tc * 512,
                                       [[EC * T, 128], [T, EC], [1, 512]])
                        _inc(g.dma_start(xnt2[tc % 2][:, :], src_), s_x2)
                        for oc in range(OCQ):
                            O = OCG(tc, oc)
                            if O >= 2:
                                g.wait_ge(s_m2.h, s_m2.exp(O - 1))
                            if O >= 1:
                                g.wait_ge(s_wc.h, s_wc.exp(O))
                            j = oc % J
                            src = bass.AP(wsc[oc // J].tensor, j * EMB * 128,
                                          [[128, 128], [128 * 128, EC], [1, 128]])
                            _inc(g.dma_start(
                                wcol[O % 2][:, :].rearrange("p (ec d) -> p ec d", d=128),
                                src), s_wc)

                @blk2.tensor
                def _(t):
                    t.wait_ge(s_ini.h, 7)
                    for tc in range(NTCH):
                        for oc in range(OCQ + 1):
                            if oc < OCQ:
                                O = OCG(tc, oc)
                                t.wait_ge(s_wc.h, s_wc.exp(O + 1))
                                if oc == 0:
                                    t.wait_ge(s_x2.h, s_x2.exp(tc + 1))
                                if O >= 2:
                                    t.wait_ge(s_e2.h, s_e2.exp(O - 1))
                                if oc < 2 and tc >= 1:
                                    t.wait_ge(s_rm.h, s_rm.exp(tc * NSUB * 32))
                                for ec in range(EC):
                                    i_ = t.matmul(pqk[O % 2][:, :],
                                                  wcol[O % 2][:, ec * 128:(ec + 1) * 128],
                                                  xnt2[tc % 2][:, ec * 512:(ec + 1) * 512],
                                                  start=(ec == 0), stop=(ec == EC - 1))
                                    if ec == EC - 1:
                                        _inc(i_, s_m2)
                            ocp = oc - 1
                            if 0 <= ocp < 2 * J:
                                Q = QKG(tc, ocp)
                                t.wait_ge(s_e2.h, s_e2.exp(OCG(tc, ocp) + 1))
                                if Q >= 2:
                                    t.wait_ge(s_te.h, s_te.exp(NSUB * (Q - 1)))
                                if ocp == 0 and tc >= 1:
                                    t.wait_ge(s_ne.h, s_ne.exp(tc * NGT))
                                for sub in range(NSUB):
                                    _inc(t.transpose(ptr2[Q % 2][:, sub * 128:(sub + 1) * 128],
                                                     qgd[Q % 2][:, sub * 128:(sub + 1) * 128],
                                                     ident[:, :]), s_t2)
                        for s in range(NSUB):
                            for g in range(8 + 2):
                                if g < 8:
                                    G = GRG(tc, s, g)
                                    if g % 4 == 0:
                                        HS = (tc * NSUB + s) * 2 + g // 4
                                        t.wait_ge(s_bl.h, s_bl.exp((HS + 1) * 2 * J))
                                    for c in range(4):
                                        C = CHG(tc, s, g, c)
                                        if C >= 2:
                                            t.wait_ge(s_rm.h, s_rm.exp(C - 1))
                                        if s == 0 and g == 0 and c < 2:
                                            t.wait_ge(s_e2.h, s_e2.exp(OCG(tc, OCQ - 1) + 1))
                                        for i in range(4):
                                            tt = g * 16 + c * 4 + i
                                            hs2 = tt // 64
                                            to = tt % 64
                                            mm = t.matmul(
                                                pqk[C % 2][:, i * 128:(i + 1) * 128],
                                                bq64[hs2 % 2][0:J, to * 128:(to + 1) * 128],
                                                bk64[hs2 % 2][0:J, to * 128:(to + 1) * 128],
                                                start=True, stop=True)
                                            if i == 3:
                                                _inc(mm, s_m1)
                                gp = g - 1
                                if 0 <= gp < 8:
                                    G = GRG(tc, s, gp)
                                    t.wait_ge(s_rm.h, s_rm.exp(4 * (G + 1)))
                                    if G >= 1:
                                        t.wait_ge(s_ne.h, s_ne.exp(G))
                                    _inc(t.transpose(ptr2[0][0:16, 0:128], nm16[G % 2][:, :],
                                                     ident[:, :]), s_nt)
                                gq = g - 2
                                if 0 <= gq < 8:
                                    G = GRG(tc, s, gq)
                                    for c in range(4):
                                        C = CHG(tc, s, gq, c)
                                        t.wait_ge(s_ex.h, s_ex.exp(C + 1))
                                        hh = c // 2
                                        HF = HFG(tc, s, gq, hh)
                                        if c % 2 == 0 and HF >= 2:
                                            t.wait_ge(s_no.h, s_no.exp(HF - 1))
                                        for i in range(4):
                                            tt = gq * 16 + c * 4 + i
                                            ttc = s * 128 + tt
                                            k = (c % 2) * 4 + i
                                            rhs = bass.AP(vall, ttc,
                                                          [[(J + 1) * TCH, 128], [TCH, J + 1], [0, 4]])
                                            mm = t.matmul(pav[HF % 2][:, k * VC:(k + 1) * VC],
                                                          expt[C % 4][:, i * 128:(i + 1) * 128],
                                                          rhs, start=True, stop=True)
                                            if i == 3:
                                                _inc(mm, s_m3)
                                if 0 <= gp < 8:
                                    G = GRG(tc, s, gp)
                                    t.wait_ge(s_in.h, s_in.exp(G + 1))
                                    for c in range(4):
                                        C = CHG(tc, s, gp, c)
                                        if C >= 2:
                                            t.wait_ge(s_ex.h, s_ex.exp(C - 1))
                                        for i in range(4):
                                            tt = gp * 16 + c * 4 + i
                                            hs2 = tt // 64
                                            to = tt % 64
                                            mm = t.matmul(
                                                pt3[C % 2][:, i * 128:(i + 1) * 128],
                                                bk64[hs2 % 2][0:J + 1, to * 128:(to + 1) * 128],
                                                bq64[hs2 % 2][0:J + 1, to * 128:(to + 1) * 128],
                                                start=True, stop=True)
                                            if i == 3:
                                                _inc(mm, s_mT)

                @blk2.vector
                def _(v):
                    for tc in range(NTCH):
                        for oc in range(OCQ):
                            O = OCG(tc, oc)
                            j = oc % J
                            v.wait_ge(s_m2.h, s_m2.exp(O + 1))
                            if oc < 2 * J:
                                Q = QKG(tc, oc)
                                if Q >= 2:
                                    v.wait_ge(s_t2.h, s_t2.exp(NSUB * (Q - 1)))
                                bias = bqg if oc < J else bkc
                                _inc(v.tensor_scalar(qgd[Q % 2][:, :], pqk[O % 2][:, :],
                                                     bias[:, j:j + 1], None, ALU.add), s_e2)
                            else:
                                if oc == 2 * J and tc >= 1:
                                    v.wait_ge(s_m3.h, s_m3.exp(tc * NSUB * 32))
                                _inc(v.tensor_scalar(vall[:, j * TCH:(j + 1) * TCH],
                                                     pqk[O % 2][:, :],
                                                     bvc[:, j:j + 1], None, ALU.add), s_e2)
                        for s in range(NSUB):
                            for g in range(8 + 2):
                                if g < 8:
                                    G = GRG(tc, s, g)
                                    for c in range(4):
                                        C = CHG(tc, s, g, c)
                                        v.wait_ge(s_m1.h, s_m1.exp(C + 1))
                                        if c == 0 and G >= 2:
                                            v.wait_ge(s_nt.h, s_nt.exp(G - 1))
                                        _inc(v.tensor_reduce(
                                            nm16[G % 2][:, c * 4:(c + 1) * 4],
                                            pqk[C % 2][:, :].rearrange("p (i e) -> p i e", i=4),
                                            AX.X, ALU.max, negate=True), s_rm)
                                gq = g - 2
                                if 0 <= gq < 8:
                                    G = GRG(tc, s, gq)
                                    for hh in range(2):
                                        HF = HFG(tc, s, gq, hh)
                                        v.wait_ge(s_m3.h, s_m3.exp(4 * G + 2 * hh + 2))
                                        rsum = bass.AP(pav[HF % 2], 4 * J,
                                                       [[512, 128], [VC, 8]])
                                        _inc(v.reciprocal(sray[HF % 2][:, :], rsum), s_rc)
                                        v.wait_ge(s_rc.h, s_rc.exp(HF + 1))
                                        hid = (gq * 2 + hh) % 8
                                        half64 = (gq * 2 + hh) // 8
                                        B = B64(tc, s, half64)
                                        if B >= 2 and hid == 0:
                                            v.wait_ge(s_as.h, s_as.exp(B - 1))
                                        for k in range(8):
                                            tok64 = hid * 8 + k
                                            ts_ = v.tensor_scalar(
                                                avb[B % 2][:, tok64 * H:(tok64 + 1) * H],
                                                pav[HF % 2][:, k * VC:k * VC + H],
                                                sray[HF % 2][:, k:k + 1], None, ALU.mult)
                                            if k == 7:
                                                _inc(ts_, s_no)

                @blk2.scalar
                def _(sc):
                    for tc in range(NTCH):
                        for ocp in range(2 * J):
                            Q = QKG(tc, ocp)
                            if ocp == 0 and tc >= 1:
                                sc.wait_ge(s_bl.h, s_bl.exp(tc * NSUB * 2 * J))
                            for sub in range(NSUB):
                                sc.wait_ge(s_t2.h, s_t2.exp(NSUB * (Q + 1)))
                                _inc(sc.copy(
                                    qgt[:, (ocp * NSUB + sub) * 128:(ocp * NSUB + sub + 1) * 128],
                                    ptr2[Q % 2][:, sub * 128:(sub + 1) * 128]), s_te)
                        for s in range(NSUB):
                            for g in range(8 + 2):
                                gp = g - 1
                                if 0 <= gp < 8:
                                    G = GRG(tc, s, gp)
                                    sc.wait_ge(s_nt.h, s_nt.exp(G + 1))
                                    if G >= 2:
                                        sc.wait_ge(s_in.h, s_in.exp(G - 1))
                                    _inc(sc.copy(nmr[G % 2][:, :], ptr2[0][0:16, 0:128]), s_ne)
                                    for c in range(4):
                                        C = CHG(tc, s, gp, c)
                                        sc.wait_ge(s_mT.h, s_mT.exp(C + 1))
                                        if C >= 4:
                                            sc.wait_ge(s_m3.h, s_m3.exp(C - 3))
                                        _inc(sc.activation(expt[C % 4][:, :], pt3[C % 2][:, :],
                                                           AF.Exp), s_ex)
                            for half64 in range(2):
                                B = B64(tc, s, half64)
                                sc.wait_ge(s_no.h, s_no.exp((B + 1) * 8))
                                if B >= 1:
                                    sc.wait_ge(s_as.h, s_as.exp(B))
                                base = tc * TCH + s * 128 + half64 * 64
                                dst = bass.AP(attn_s.tensor, base * EMB,
                                              [[H, 128], [EMB, 64], [1, H]])
                                _inc(sc.dma_start(dst, avb[B % 2][:, :].rearrange(
                                    "p (t h) -> p t h", h=H)), s_as)

                @blk2.sync
                def _(s):
                    for tc in range(NTCH):
                        for sub in range(NSUB):
                            SB = tc * NSUB + sub
                            for hs in range(2):
                                HS = SB * 2 + hs
                                s.wait_ge(s_te.h, s_te.exp((tc * 2 * J + 2 * J) * NSUB))
                                if HS >= 1:
                                    s.wait_ge(s_bl.h, s_bl.exp(HS * 2 * J))
                                if HS >= 2:
                                    s.wait_ge(s_mT.h, s_mT.exp((HS - 1) * 16))
                                for j in range(J):
                                    src = qgt[hs * 64:hs * 64 + 64,
                                              (j * NSUB + sub) * 128:(j * NSUB + sub + 1) * 128]
                                    dst = bass.AP(bq64[hs % 2], j * 64 * 128,
                                                  [[64 * 128, 1], [128, 64], [1, 128]])
                                    _inc(s.dma_start(dst, src), s_bl)
                                for j in range(J):
                                    src = qgt[hs * 64:hs * 64 + 64,
                                              ((J + j) * NSUB + sub) * 128:((J + j) * NSUB + sub + 1) * 128]
                                    dst = bass.AP(bk64[hs % 2], j * 64 * 128,
                                                  [[64 * 128, 1], [128, 64], [1, 128]])
                                    _inc(s.dma_start(dst, src), s_bl)
                            for g in range(8):
                                G = GRG(tc, sub, g)
                                s.wait_ge(s_ne.h, s_ne.exp(G + 1))
                                if G >= 1:
                                    s.wait_ge(s_in.h, s_in.exp(G))
                                dst = bass.AP(bq64[(g // 4) % 2],
                                              J * 64 * 128 + (g % 4) * 16 * 128,
                                              [[64 * 128, 1], [128, 16], [1, 128]])
                                _inc(s.dma_start(dst, nmr[G % 2][:, :]), s_in)

            nc.all_engine_barrier()

        if phases < 3:
            return nc
        with ExitStack() as p4:
            wdT = p4.enter_context(nc.sbuf_tensor("wdT", [128, NF * EP], BF16))
            ptsb = [p4.enter_context(nc.sbuf_tensor(f"ptsb{i}", [128, NF * 128], BF16)) for i in range(2)]
            xsl = [p4.enter_context(nc.sbuf_tensor(f"xsl{i}", [128, EP], F32)) for i in range(2)]
            asl = [p4.enter_context(nc.sbuf_tensor(f"asl{i}", [128, EP], F32)) for i in range(2)]
            osb = [p4.enter_context(nc.sbuf_tensor(f"osb{i}", [128, EP], F32)) for i in range(2)]
            py = [p4.enter_context(nc.psum_tensor(f"py{i}", [128, EP], F32)) for i in range(2)]

            blk4 = p4.enter_context(nc.Block("p4"))

            @blk4.sync
            def _(s):
                s.wait_ge(s_cd.h, s_cd.exp(EMB // 256))
                WB = 8
                for p in range(NEP):
                    if p >= 1:
                        s.wait_ge(s_my.h, s_my.exp(p * NTG))
                    for fc in range(NF):
                        if fc % WB == 0 and p * NF + fc > 0:
                            s.wait_ge(s_wt.h, s_wt.exp(p * NF + fc))
                        _inc(s.dma_start_transpose(
                            wdT[:, fc * EP:(fc + 1) * EP],
                            wdb[p * EP:(p + 1) * EP, fc * 128:(fc + 1) * 128]), s_wt)
                    for tg in range(NTG):
                        PTG = p * NTG + tg
                        s.wait_ge(s_cb.h, s_cb.exp(2 * (PTG + 1)))
                        if PTG >= 1:
                            s.wait_ge(s_ou.h, s_ou.exp(PTG))
                        _inc(s.dma_start(out[tg * 128:(tg + 1) * 128, p * EP:(p + 1) * EP],
                                         osb[PTG % 2][:, :]), s_ou)
                s.wait_ge(s_ou.h, s_ou.exp(NEP * NTG))

            @blk4.gpsimd
            def _(g):
                g.wait_ge(s_pt.h, s_pt.exp(NF))
                NH = NF // 2
                for p in range(NEP):
                    for tg in range(NTG):
                        PTG = p * NTG + tg
                        if PTG >= 2:
                            g.wait_ge(s_my.h, s_my.exp(PTG - 1))
                        if PTG >= 1:
                            g.wait_ge(s_p4.h, s_p4.exp(2 * PTG))
                        for hf in range(2):
                            src = bass.AP(pts.tensor, hf * NH * T * 128 + tg * 128 * 128,
                                          [[128, 128], [T * 128, NH], [1, 128]])
                            dst = ptsb[PTG % 2][:, hf * NH * 128:(hf + 1) * NH * 128].rearrange(
                                "p (f t) -> p f t", t=128)
                            _inc(g.dma_start(dst, src), s_p4)

            @blk4.scalar
            def _(sc):
                sc.wait_ge(s_as.h, s_as.exp(T // 64))
                for p in range(NEP):
                    for tg in range(NTG):
                        PTG = p * NTG + tg
                        if PTG >= 2:
                            sc.wait_ge(s_cb.h, s_cb.exp(2 * (PTG - 1)))
                        if PTG >= 1:
                            sc.wait_ge(s_xa.h, s_xa.exp(2 * PTG))
                        _inc(sc.dma_start(xsl[PTG % 2][:, :],
                                          x_in[tg * 128:(tg + 1) * 128, p * EP:(p + 1) * EP]), s_xa)
                        _inc(sc.dma_start(asl[PTG % 2][:, :],
                                          attn_s[tg * 128:(tg + 1) * 128, p * EP:(p + 1) * EP]), s_xa)

            @blk4.tensor
            def _(t):
                for p in range(NEP):
                    for tg in range(NTG):
                        PTG = p * NTG + tg
                        t.wait_ge(s_p4.h, s_p4.exp(2 * (PTG + 1)))
                        if PTG >= 2:
                            t.wait_ge(s_cb.h, s_cb.exp(2 * (PTG - 1)))
                        for fc in range(NF):
                            if tg == 0 and fc % 8 == 0:
                                t.wait_ge(s_wt.h, s_wt.exp(p * NF + min(NF, fc + 8)))
                            i_ = t.matmul(py[PTG % 2][:, :],
                                          ptsb[PTG % 2][:, fc * 128:(fc + 1) * 128],
                                          wdT[:, fc * EP:(fc + 1) * EP],
                                          start=(fc == 0), stop=(fc == NF - 1))
                            if fc == NF - 1:
                                _inc(i_, s_my)

            @blk4.vector
            def _(v):
                for p in range(NEP):
                    for tg in range(NTG):
                        PTG = p * NTG + tg
                        v.wait_ge(s_my.h, s_my.exp(PTG + 1))
                        v.wait_ge(s_xa.h, s_xa.exp(2 * (PTG + 1)))
                        if PTG >= 2:
                            v.wait_ge(s_ou.h, s_ou.exp(PTG - 1))
                        _inc(v.tensor_add(osb[PTG % 2][:, :], xsl[PTG % 2][:, :],
                                          py[PTG % 2][:, :]), s_cb)
                        v.wait_ge(s_cb.h, s_cb.exp(2 * PTG + 1))
                        _inc(v.tensor_add(osb[PTG % 2][:, :], osb[PTG % 2][:, :],
                                          asl[PTG % 2][:, :]), s_cb)

    return nc


_PROG_CACHE = {}


def _get_prog(T, EMB, FF):
    key = (T, EMB, FF)
    if key not in _PROG_CACHE:
        _PROG_CACHE[key] = build_program(T, EMB, FF)
    return _PROG_CACHE[key]


def kernel(L0_attn_x, L0_attn_Wq, L0_attn_bq, L0_attn_Wk, L0_attn_bk,
           L0_attn_Wv, L0_attn_bv, L0_ffn_W_gate, L0_ffn_W_up, L0_ffn_W_down):
    x = np.ascontiguousarray(np.asarray(L0_attn_x, dtype=np.float32))
    B, S, EMB = x.shape
    NTOK = B * S
    T = NTOK // N_CORES
    x2 = x.reshape(NTOK, EMB)
    wq2 = np.ascontiguousarray(np.asarray(L0_attn_Wq, dtype=np.float32).reshape(EMB, -1))
    wk2 = np.ascontiguousarray(np.asarray(L0_attn_Wk, dtype=np.float32).reshape(EMB, -1))
    wv2 = np.ascontiguousarray(np.asarray(L0_attn_Wv, dtype=np.float32).reshape(EMB, -1))
    bq2 = np.ascontiguousarray(np.asarray(L0_attn_bq, dtype=np.float32))
    bk2 = np.ascontiguousarray(np.asarray(L0_attn_bk, dtype=np.float32))
    bv2 = np.ascontiguousarray(np.asarray(L0_attn_bv, dtype=np.float32))
    wg2 = np.ascontiguousarray(np.asarray(L0_ffn_W_gate, dtype=np.float32))
    wu2 = np.ascontiguousarray(np.asarray(L0_ffn_W_up, dtype=np.float32))
    wd2 = np.ascontiguousarray(np.asarray(L0_ffn_W_down, dtype=np.float32))
    FF = wg2.shape[0]

    nc = _get_prog(T, EMB, FF)
    in_maps = []
    for c in range(N_CORES):
        in_maps.append({
            "x": x2[c * T:(c + 1) * T],
            "wq": wq2, "wk": wk2, "wv": wv2,
            "bq": bq2, "bk": bk2, "bv": bv2,
            "wg": wg2, "wu": wu2, "wd": wd2,
        })
    res = run_bass_kernel_spmd(nc, in_maps, core_ids=list(range(N_CORES)))
    outs = [np.asarray(res.results[c]["out"]) for c in range(N_CORES)]
    return np.concatenate(outs, axis=0).reshape(B, S, EMB).astype(np.float32)


# version 10
# speedup vs baseline: 1.0926x; 1.0243x over previous
import numpy as np
import ml_dtypes
from contextlib import ExitStack

import concourse.bass as bass
import concourse.mybir as mybir
from concourse.bass_utils import run_bass_kernel_spmd

F32 = mybir.dt.float32
BF16 = mybir.dt.bfloat16
AF = mybir.ActivationFunctionType
ALU = mybir.AluOpType
AX = mybir.AxisListType

N_CORES = 8


class SemT:
    def __init__(self, handle, unit):
        self.h = handle
        self.unit = unit

    def exp(self, k):
        return k * self.unit


def _inc(inst, sem):
    inst.then_inc(sem.h, sem.unit)
    return inst


def build_program(T=2048, EMB=4096, FF=11008, detect_races=False, phases=4, p0mask=15, p1mask=15):
    assert EMB % 512 == 0 and T % 512 == 0 and FF % 256 == 0
    EC = EMB // 128
    H = EMB // 128
    J = H // 4
    NF = FF // 128
    TCH = 512
    NTCH = T // TCH
    NSUB = TCH // 128
    NQ1 = T // 512
    NT0 = T // 128
    NTG = T // 128
    EP = 512
    NEP = EMB // EP
    NGT = NSUB * 8
    VC = 4 * (J + 1)
    SCALE = 1.0 / float(np.sqrt(128.0))
    OCQ = 3 * J
    PVC = 3 + 3 * J

    nc = bass.Bass("TRN2", target_bir_lowering=False,
                   detect_race_conditions=detect_races)

    x_in = nc.dram_tensor("x", [T, EMB], F32, kind="ExternalInput").ap()
    wq = nc.dram_tensor("wq", [EMB, EMB], F32, kind="ExternalInput").ap()
    wk = nc.dram_tensor("wk", [EMB, 128 * J], F32, kind="ExternalInput").ap()
    wv = nc.dram_tensor("wv", [EMB, 128 * J], F32, kind="ExternalInput").ap()
    bq = nc.dram_tensor("bq", [128, H], F32, kind="ExternalInput").ap()
    bk = nc.dram_tensor("bk", [128, J], F32, kind="ExternalInput").ap()
    bv = nc.dram_tensor("bv", [128, J], F32, kind="ExternalInput").ap()
    wg = nc.dram_tensor("wg", [FF, EMB], F32, kind="ExternalInput").ap()
    wu = nc.dram_tensor("wu", [FF, EMB], F32, kind="ExternalInput").ap()
    wd = nc.dram_tensor("wd", [EMB, FF], F32, kind="ExternalInput").ap()
    out = nc.dram_tensor("out", [T, EMB], F32, kind="ExternalOutput").ap()

    wgb = nc.dram_tensor("wgb", [FF, EMB], BF16, kind="Internal").ap()
    wub = nc.dram_tensor("wub", [FF, EMB], BF16, kind="Internal").ap()
    wdb = nc.dram_tensor("wdb", [EMB, FF], BF16, kind="Internal").ap()
    wqgb = nc.dram_tensor("wqgb", [J * EMB * 128], BF16, kind="Internal").ap()
    wkb = nc.dram_tensor("wkb", [J * EMB * 128], BF16, kind="Internal").ap()
    wvb = nc.dram_tensor("wvb", [J * EMB * 128], BF16, kind="Internal").ap()
    pts = nc.dram_tensor("pts", [NF * T * 128], BF16, kind="Internal").ap()
    xnb = nc.dram_tensor("xnb", [128, EC * T], BF16, kind="Internal").ap()
    attn_s = nc.dram_tensor("attn_s", [T, EMB], F32, kind="Internal").ap()

    with ExitStack() as root:
        def SEM(name, unit=1):
            return SemT(root.enter_context(nc.semaphore(name)), unit)

        s_x = SEM("s_x", 16)
        s_ln = SEM("s_ln")
        s_sc0 = SEM("s_sc0")
        s_xt = SEM("s_xt")
        s_xe = SEM("s_xe")
        s_cg = SEM("s_cg", 16)
        s_cu = SEM("s_cu", 16)
        s_cd = SEM("s_cd", 16)
        s_pq = SEM("s_pq", 16)
        s_pk = SEM("s_pk", 16)
        s_sk = SEM("s_sk", 16)
        s_sv = SEM("s_sv", 16)
        s_pv = SEM("s_pv")
        s_pw = SEM("s_pw", 16)
        s_ini = SEM("s_ini")
        s_tr = SEM("s_tr", 16)
        s_mg = SEM("s_mg")
        s_mu = SEM("s_mu")
        s_mf = SEM("s_mf")
        s_si = SEM("s_si")
        s_ml = SEM("s_ml")
        s_pt = SEM("s_pt", 16)
        s_xb = SEM("s_xb", 16)
        s_x2 = SEM("s_x2", 16)
        s_wc = SEM("s_wc", 16)
        s_m2 = SEM("s_m2")
        s_e2 = SEM("s_e2")
        s_t2 = SEM("s_t2")
        s_te = SEM("s_te")
        s_bl = SEM("s_bl", 16)
        s_m1 = SEM("s_m1")
        s_rm = SEM("s_rm")
        s_nt = SEM("s_nt")
        s_ne = SEM("s_ne")
        s_in = SEM("s_in", 16)
        s_mT = SEM("s_mT")
        s_ex = SEM("s_ex")
        s_m3 = SEM("s_m3")
        s_rc = SEM("s_rc")
        s_no = SEM("s_no")
        s_as = SEM("s_as", 16)
        s_wt = SEM("s_wt", 16)
        s_p4 = [SEM(f"s_p4_{i}", 16) for i in range(3)]
        s_xa = SEM("s_xa", 16)
        s_my = SEM("s_my")
        s_cb = SEM("s_cb")
        s_ou = SEM("s_ou", 16)

        ident = root.enter_context(nc.sbuf_tensor("ident", [128, 128], BF16))
        bqg = root.enter_context(nc.sbuf_tensor("bqg", [128, J], F32))
        bkc = root.enter_context(nc.sbuf_tensor("bkc", [128, J], F32))
        bvc = root.enter_context(nc.sbuf_tensor("bvc", [128, J], F32))
        bqt = root.enter_context(nc.sbuf_tensor("bqt", [128, H], F32))
        bq2 = root.enter_context(nc.sbuf_tensor("bq2", [128, J], F32))

        with ExitStack() as scope_x:
            xnT = scope_x.enter_context(nc.sbuf_tensor("xnT", [128, EC * T], BF16))

            with ExitStack() as p0:
                xt = [p0.enter_context(nc.sbuf_tensor(f"xt{i}", [128, EMB], F32)) for i in range(2)]
                xsq = p0.enter_context(nc.sbuf_tensor("xsq", [128, EMB], BF16))
                xn = p0.enter_context(nc.sbuf_tensor("xn", [128, EMB], BF16))
                stt = [p0.enter_context(nc.sbuf_tensor(f"stt{i}", [128, 8], F32)) for i in range(2)]
                pqf = p0.enter_context(nc.sbuf_tensor("pqf", [128, EMB], F32))
                prd = p0.enter_context(nc.sbuf_tensor("prd", [128, EMB // 4], F32))
                pjq = p0.enter_context(nc.sbuf_tensor("pjq", [128, 128 * J], BF16))
                skb = p0.enter_context(nc.sbuf_tensor("skb", [128, 128 * J], BF16))
                svb = p0.enter_context(nc.sbuf_tensor("svb", [128, 128 * J], BF16))
                pjk = p0.enter_context(nc.sbuf_tensor("pjk", [128, 128 * J], BF16))
                pjv = p0.enter_context(nc.sbuf_tensor("pjv", [128, 128 * J], BF16))
                ptx = [p0.enter_context(nc.psum_tensor(f"ptx{i}", [128, 512], BF16)) for i in range(2)]

                blk0 = p0.enter_context(nc.Block("p0"))

                @blk0.sync
                def _(s):
                    for i in range(NT0 if (p0mask & 1) else 0):
                        if i >= 2:
                            s.wait_ge(s_sc0.h, s_sc0.exp(2 * (i - 2) + 1))
                            s.wait_ge(s_ln.h, s_ln.exp(7 * (i - 2) + 7))
                        _inc(s.dma_start(xt[i % 2][:, :], x_in[i * 128:(i + 1) * 128, :]), s_x)

                @blk0.scalar
                def _(sc):
                    for i in range(NT0 if (p0mask & 1) else 0):
                        p = i % 2
                        st = stt[p]
                        sc.wait_ge(s_x.h, s_x.exp(i + 1))
                        if i >= 2:
                            sc.wait_ge(s_ln.h, s_ln.exp(7 * (i - 2) + 7))
                        _inc(sc.activation(xsq[:, :], xt[i % 2][:, :], AF.Square,
                                           accum_out=st[:, 1:2]), s_sc0)
                        sc.wait_ge(s_ln.h, s_ln.exp(7 * i + 5))
                        _inc(sc.activation(st[:, 6:7], st[:, 5:6], AF.Sqrt), s_sc0)

                @blk0.vector
                def _(v):
                    if not (p0mask & 8):
                        return
                    v.wait_ge(s_pk.h, s_pk.exp(3))
                    q4 = bqt[:, :].rearrange("p (h r) -> p h r", r=4)
                    _inc(v.tensor_add(bq2[:, :], q4[:, :, 0], q4[:, :, 1]), s_pv)
                    _inc(v.tensor_add(bqg[:, :], q4[:, :, 2], q4[:, :, 3]), s_pv)
                    v.wait_ge(s_pv.h, s_pv.exp(2))
                    _inc(v.tensor_add(bqg[:, :], bqg[:, :], bq2[:, :]), s_pv)
                    v.wait_ge(s_pv.h, s_pv.exp(3))
                    _inc(v.tensor_scalar_mul(bqg[:, :], bqg[:, :], SCALE), s_pv)
                    def _prep_chain(ec):
                        base = 4 + PVC * ec
                        v.wait_ge(s_pq.h, s_pq.exp(ec + 1))
                        if ec >= 1:
                            v.wait_ge(s_pw.h, s_pw.exp(3 * ec))
                        q4 = pqf[:, :].rearrange("p (c r) -> p c r", r=4)
                        _inc(v.tensor_add(prd[:, :], q4[:, :, 0], q4[:, :, 1]), s_pv)
                        v.wait_ge(s_pv.h, s_pv.exp(base + 1))
                        _inc(v.tensor_add(prd[:, :], prd[:, :], q4[:, :, 2]), s_pv)
                        v.wait_ge(s_pv.h, s_pv.exp(base + 2))
                        _inc(v.tensor_add(prd[:, :], prd[:, :], q4[:, :, 3]), s_pv)
                        v.wait_ge(s_pv.h, s_pv.exp(base + 3))
                        rj = prd[:, :].rearrange("p (d j) -> p d j", j=J)
                        for j in range(J):
                            _inc(v.tensor_scalar_mul(pjq[:, j * 128:(j + 1) * 128],
                                                     rj[:, :, j], SCALE), s_pv)
                        v.wait_ge(s_sk.h, s_sk.exp(ec + 1))
                        if ec >= 1:
                            v.wait_ge(s_pw.h, s_pw.exp(3 * ec))
                        rk = skb[:, :].rearrange("p (e j) -> p e j", j=J)
                        for j in range(J):
                            _inc(v.tensor_copy(pjk[:, j * 128:(j + 1) * 128], rk[:, :, j]), s_pv)
                        v.wait_ge(s_sv.h, s_sv.exp(ec + 1))
                        if ec >= 1:
                            v.wait_ge(s_pw.h, s_pw.exp(3 * ec))
                        rv = svb[:, :].rearrange("p (e j) -> p e j", j=J)
                        for j in range(J):
                            _inc(v.tensor_copy(pjv[:, j * 128:(j + 1) * 128], rv[:, :, j]), s_pv)
                    for i in range(NT0 if (p0mask & 1) else 0):
                        p = i % 2
                        st = stt[p]
                        v.wait_ge(s_x.h, s_x.exp(i + 1))
                        if i >= 2:
                            v.wait_ge(s_sc0.h, s_sc0.exp(2 * (i - 2) + 2))
                        _inc(v.reduce_sum(st[:, 0:1], xt[i % 2][:, :], AX.X), s_ln)
                        v.wait_ge(s_sc0.h, s_sc0.exp(2 * i + 1))
                        v.wait_ge(s_ln.h, s_ln.exp(7 * i + 1))
                        _inc(v.tensor_scalar_mul(st[:, 2:3], st[:, 0:1], 1.0 / EMB), s_ln)
                        _inc(v.tensor_scalar_mul(st[:, 3:4], st[:, 1:2], 1.0 / EMB), s_ln)
                        v.wait_ge(s_ln.h, s_ln.exp(7 * i + 3))
                        _inc(v.tensor_mul(st[:, 4:5], st[:, 2:3], st[:, 2:3]), s_ln)
                        v.wait_ge(s_ln.h, s_ln.exp(7 * i + 4))
                        _inc(v.tensor_scalar(st[:, 5:6], st[:, 3:4], st[:, 4:5], 1e-5,
                                             ALU.subtract, ALU.add), s_ln)
                        v.wait_ge(s_sc0.h, s_sc0.exp(2 * i + 2))
                        _inc(v.reciprocal(st[:, 6:7], st[:, 6:7]), s_ln)
                        if i >= 1 and not (p0mask & 16):
                            for ec in range(EC):
                                gidx = (i - 1) * EC + ec
                                bank = (gidx // 4) % 2
                                slot = gidx % 4
                                v.wait_ge(s_xt.h, s_xt.exp(4 * (gidx // 4 + 1)))
                                _inc(v.tensor_copy(
                                    xnT[:, ec * T + (i - 1) * 128:ec * T + i * 128],
                                    ptx[bank][:, slot * 128:(slot + 1) * 128]), s_xe)
                        v.wait_ge(s_ln.h, s_ln.exp(7 * i + 6))
                        if i >= 1 and not (p0mask & 16):
                            v.wait_ge(s_xt.h, s_xt.exp(EC * i))
                        _inc(v.tensor_scalar(xn[:, :], xt[i % 2][:, :], st[:, 2:3], st[:, 6:7],
                                             ALU.subtract, ALU.mult), s_ln)
                        if p0mask & 4:
                            for ec in (2 * i, 2 * i + 1):
                                if ec < EC:
                                    _prep_chain(ec)
                    for ec in range(EC if (p0mask & 1) and not (p0mask & 16) else 0):
                        gidx = (NT0 - 1) * EC + ec
                        bank = (gidx // 4) % 2
                        slot = gidx % 4
                        v.wait_ge(s_xt.h, s_xt.exp(4 * (gidx // 4 + 1)))
                        _inc(v.tensor_copy(
                            xnT[:, ec * T + (NT0 - 1) * 128:ec * T + NT0 * 128],
                            ptx[bank][:, slot * 128:(slot + 1) * 128]), s_xe)
                    if p0mask & 4:
                        for ec in range(2 * NT0, EC):
                            _prep_chain(ec)

                @blk0.gpsimd
                def _(g):
                    _inc(g.memset(ident[:, :], 0.0), s_ini)
                    g.wait_ge(s_ini.h, 1)
                    _inc(g.affine_select(out=ident[:, :], in_=ident[:, :],
                                         compare_op=ALU.not_equal, fill=1.0, base=0,
                                         pattern=[[-1, 128]], channel_multiplier=1), s_ini)
                    _inc(g.dma_start(bqt[:, :], bq[:, :]), s_pk)
                    _inc(g.dma_start(bkc[:, :], bk[:, :]), s_pk)
                    _inc(g.dma_start(bvc[:, :], bv[:, :]), s_pk)
                    def _emit_writes(ec):
                        base = 4 + PVC * ec
                        if ec >= 1:
                            g.wait_ge(s_pw.h, s_pw.exp(3 * ec))
                        g.wait_ge(s_pv.h, s_pv.exp(base + 3 + J))
                        dst = bass.AP(wqgb.tensor, ec * 128 * 128,
                                      [[128, 128], [EMB * 128, J], [1, 128]])
                        _inc(g.dma_start(dst, pjq[:, :].rearrange("p (j d) -> p j d", d=128)), s_pw)
                        g.wait_ge(s_pv.h, s_pv.exp(base + 3 + 2 * J))
                        dst = bass.AP(wkb.tensor, ec * 128 * 128,
                                      [[128, 128], [EMB * 128, J], [1, 128]])
                        _inc(g.dma_start(dst, pjk[:, :].rearrange("p (j d) -> p j d", d=128)), s_pw)
                        g.wait_ge(s_pv.h, s_pv.exp(base + 3 + 3 * J))
                        dst = bass.AP(wvb.tensor, ec * 128 * 128,
                                      [[128, 128], [EMB * 128, J], [1, 128]])
                        _inc(g.dma_start(dst, pjv[:, :].rearrange("p (j d) -> p j d", d=128)), s_pw)
                    for ec in range(EC if (p0mask & 4) else 0):
                        if ec >= 1:
                            _emit_writes(ec - 1)
                            g.wait_ge(s_pv.h, s_pv.exp(4 + PVC * ec))
                        _inc(g.dma_start(pqf[:, :], wq[ec * 128:(ec + 1) * 128, :]), s_pq)
                        _inc(g.dma_start(skb[:, :], wk[ec * 128:(ec + 1) * 128, :]), s_sk)
                        _inc(g.dma_start(svb[:, :], wv[ec * 128:(ec + 1) * 128, :]), s_sv)
                    if p0mask & 4:
                        _emit_writes(EC - 1)

                @blk0.tensor
                def _(t):
                    t.wait_ge(s_ini.h, 2)
                    for i in range(NT0 if (p0mask & 1) and not (p0mask & 16) else 0):
                        t.wait_ge(s_ln.h, s_ln.exp(7 * i + 7))
                        for ec in range(EC):
                            gidx = i * EC + ec
                            grp = gidx // 4
                            bank = grp % 2
                            slot = gidx % 4
                            if slot == 0 and grp >= 2:
                                t.wait_ge(s_xe.h, s_xe.exp(4 * (grp - 1)))
                            _inc(t.transpose(ptx[bank][:, slot * 128:(slot + 1) * 128],
                                             xn[:, ec * 128:(ec + 1) * 128],
                                             ident[:, :]), s_xt)

            nc.all_engine_barrier()

            if phases < 1:
                return nc
            with ExitStack() as p1:
                wgT = [p1.enter_context(nc.sbuf_tensor(f"wgT{i}", [128, EC * 256], BF16)) for i in range(2)]
                wuT = [p1.enter_context(nc.sbuf_tensor(f"wuT{i}", [128, EC * 256], BF16)) for i in range(2)]
                sg = [p1.enter_context(nc.sbuf_tensor(f"sg{i}", [128, 512], F32)) for i in range(2)]
                ptt = [p1.enter_context(nc.sbuf_tensor(f"ptt{i}", [128, T], BF16)) for i in range(2)]
                pg = [p1.enter_context(nc.psum_tensor(f"pg{i}", [128, 512], F32)) for i in range(2)]
                pu = [p1.enter_context(nc.psum_tensor(f"pu{i}", [128, 512], F32)) for i in range(2)]

                blk1 = p1.enter_context(nc.Block("p1"))

                NP = NF // 2

                @blk1.sync
                def _(s):
                    for P in range(NP if (p1mask & 1) else 0):
                        s.wait_ge(s_cg.h, s_cg.exp(P + 1))
                        s.wait_ge(s_cu.h, s_cu.exp(P + 1))
                        if P >= 2 and (p1mask & 2):
                            s.wait_ge(s_mu.h, s_mu.exp(NQ1 * (2 * P - 2)))
                        for k in range(2 * EC):
                            if k % 8 == 0 and 2 * EC * P + k > 0:
                                s.wait_ge(s_tr.h, s_tr.exp(2 * EC * P + k))
                            if k < EC:
                                _inc(s.dma_start_transpose(
                                    wgT[P % 2][:, k * 256:(k + 1) * 256],
                                    wgb[P * 256:(P + 1) * 256, k * 128:(k + 1) * 128]), s_tr)
                            else:
                                ec = k - EC
                                _inc(s.dma_start_transpose(
                                    wuT[P % 2][:, ec * 256:(ec + 1) * 256],
                                    wub[P * 256:(P + 1) * 256, ec * 128:(ec + 1) * 128]), s_tr)

                @blk1.tensor
                def _(t):
                    for fc in range(NF if (p1mask & 2) else 0):
                        P = fc // 2
                        first = fc % 2 == 0
                        if not first:
                            t.wait_ge(s_tr.h, s_tr.exp(2 * EC * (P + 1)))
                        for tq in range(NQ1):
                            gi = fc * NQ1 + tq
                            par = gi % 2
                            if gi >= 2 and (p1mask & 4):
                                t.wait_ge(s_si.h, s_si.exp(gi - 1))
                            if gi >= 2 and (p1mask & 8):
                                t.wait_ge(s_ml.h, s_ml.exp(gi - 1))
                            fo = (fc % 2) * 128
                            for ec in range(EC):
                                if first and tq == 0 and ec % 8 == 0:
                                    t.wait_ge(s_tr.h, s_tr.exp(2 * EC * P + min(2 * EC, ec + 8)))
                                i_ = t.matmul(pg[par][:, :],
                                              wgT[(fc // 2) % 2][:, ec * 256 + fo:ec * 256 + fo + 128],
                                              xnT[:, ec * T + tq * 512:ec * T + (tq + 1) * 512],
                                              start=(ec == 0), stop=(ec == EC - 1))
                                if ec == EC - 1:
                                    _inc(i_, s_mg)
                            for ec in range(EC):
                                if first and tq == 0 and ec % 8 == 0:
                                    t.wait_ge(s_tr.h, s_tr.exp(2 * EC * P + min(2 * EC, EC + ec + 8)))
                                i_ = t.matmul(pu[par][:, :],
                                              wuT[(fc // 2) % 2][:, ec * 256 + fo:ec * 256 + fo + 128],
                                              xnT[:, ec * T + tq * 512:ec * T + (tq + 1) * 512],
                                              start=(ec == 0), stop=(ec == EC - 1))
                                if ec == EC - 1:
                                    _inc(i_, s_mu)

                @blk1.scalar
                def _(sc):
                    for fc in range(NF if (p1mask & 4) else 0):
                        for tq in range(NQ1):
                            gi = fc * NQ1 + tq
                            par = gi % 2
                            sc.wait_ge(s_mg.h, s_mg.exp(gi + 1))
                            if gi >= 2 and (p1mask & 8):
                                sc.wait_ge(s_ml.h, s_ml.exp(gi - 1))
                            _inc(sc.activation(sg[par][:, :], pg[par][:, :], AF.Silu), s_si)

                @blk1.vector
                def _(v):
                    for fc in range(NF if (p1mask & 8) else 0):
                        for tq in range(NQ1):
                            gi = fc * NQ1 + tq
                            par = gi % 2
                            v.wait_ge(s_si.h, s_si.exp(gi + 1))
                            v.wait_ge(s_mu.h, s_mu.exp(gi + 1))
                            if fc >= 2 and tq == 0:
                                v.wait_ge(s_pt.h, s_pt.exp(fc - 1))
                            _inc(v.tensor_tensor(ptt[fc % 2][:, tq * 512:(tq + 1) * 512],
                                                 sg[par][:, :], pu[par][:, :], ALU.mult), s_ml)

                @blk1.gpsimd
                def _(g):
                    ncg = FF // 256
                    ncd = EMB // 256
                    q4l = EC * T // 4
                    nsp = 0
                    ncast = 0

                    def _cast(c):
                        if c >= 1:
                            g.wait_ge(s_cg.h, s_cg.exp(c))
                            g.wait_ge(s_cu.h, s_cu.exp(c))
                        _inc(g.dma_start(wgb[c * 256:(c + 1) * 256, :],
                                         wg[c * 256:(c + 1) * 256, :]), s_cg)
                        _inc(g.dma_start(wub[c * 256:(c + 1) * 256, :],
                                         wu[c * 256:(c + 1) * 256, :]), s_cu)

                    for c in range(min(3, ncg)):
                        _cast(c)
                        ncast = c + 1
                    nwd = 0
                    for fc in range(NF if (p1mask & 8) else 0):
                        if fc % 2 == 0 and ncast < ncg:
                            _cast(ncast)
                            ncast += 1
                        if fc % 2 == 1 and fc >= 11 and nwd < ncd:
                            d = nwd
                            if d >= 1:
                                g.wait_ge(s_cd.h, s_cd.exp(d))
                            _inc(g.dma_start(wdb[d * 256:(d + 1) * 256, :],
                                             wd[d * 256:(d + 1) * 256, :]), s_cd)
                            nwd += 1
                        if fc in (2, 4, 6, 8):
                            qq = nsp
                            if qq >= 1:
                                g.wait_ge(s_xb.h, s_xb.exp(qq))
                            _inc(g.dma_start(xnb[:, qq * q4l:(qq + 1) * q4l],
                                             xnT[:, qq * q4l:(qq + 1) * q4l]), s_xb)
                            nsp += 1
                        g.wait_ge(s_ml.h, s_ml.exp(NQ1 * (fc + 1)))
                        if fc >= 1:
                            g.wait_ge(s_pt.h, s_pt.exp(fc))
                        src = ptt[fc % 2][:, :].rearrange("p (tg t) -> p tg t", t=128)
                        dst = bass.AP(pts.tensor, fc * (T * 128),
                                      [[128, 128], [128 * 128, T // 128], [1, 128]])
                        _inc(g.dma_start(dst, src), s_pt)
                    while ncast < ncg:
                        _cast(ncast)
                        ncast += 1
                    while nsp < 4:
                        qq = nsp
                        if qq >= 1:
                            g.wait_ge(s_xb.h, s_xb.exp(qq))
                        _inc(g.dma_start(xnb[:, qq * q4l:(qq + 1) * q4l],
                                         xnT[:, qq * q4l:(qq + 1) * q4l]), s_xb)
                        nsp += 1
                    for d in range(nwd, ncd):
                        if d >= 1:
                            g.wait_ge(s_cd.h, s_cd.exp(d))
                        _inc(g.dma_start(wdb[d * 256:(d + 1) * 256, :],
                                         wd[d * 256:(d + 1) * 256, :]), s_cd)

            nc.all_engine_barrier()

        if phases < 2:
            return nc
        if True:
            with ExitStack() as p23:
                wcol = [p23.enter_context(nc.sbuf_tensor(f"wcol{i}", [128, EC * 128], BF16)) for i in range(2)]
                qgd = [p23.enter_context(nc.sbuf_tensor(f"qgd{i}", [128, 512], BF16)) for i in range(2)]
                qgt = p23.enter_context(nc.sbuf_tensor("qgt", [128, 2 * J * NSUB * 128], BF16))
                vall = p23.enter_context(nc.sbuf_tensor("vall", [128, (J + 1) * TCH], BF16))
                xnt2 = [p23.enter_context(nc.sbuf_tensor(f"xnt2_{i}", [128, EC * 512], BF16)) for i in range(2)]
                bq64 = [p23.enter_context(nc.sbuf_tensor(f"bq64_{i}", [J + 1, 64 * 128], BF16)) for i in range(2)]
                bk64 = [p23.enter_context(nc.sbuf_tensor(f"bk64_{i}", [J + 1, 64 * 128], BF16)) for i in range(2)]
                nm16 = [p23.enter_context(nc.sbuf_tensor(f"nm16_{i}", [128, 16], BF16)) for i in range(2)]
                nmr = [p23.enter_context(nc.sbuf_tensor(f"nmr{i}", [16, 128], BF16)) for i in range(2)]
                expt = [p23.enter_context(nc.sbuf_tensor(f"expt{i}", [128, 512], BF16)) for i in range(4)]
                sray = [p23.enter_context(nc.sbuf_tensor(f"sray{i}", [128, 8], F32)) for i in range(2)]
                avb = [p23.enter_context(nc.sbuf_tensor(f"avb{i}", [128, 64 * H], F32)) for i in range(2)]
                pqk = [p23.enter_context(nc.psum_tensor(f"pqk{i}", [128, 512], F32)) for i in range(2)]
                pt3 = [p23.enter_context(nc.psum_tensor(f"pt3{i}", [128, 512], F32)) for i in range(2)]
                pav = [p23.enter_context(nc.psum_tensor(f"pav{i}", [128, 512], F32)) for i in range(2)]
                ptr2 = [p23.enter_context(nc.psum_tensor(f"ptr2{i}", [128, 512], BF16)) for i in range(2)]

                blk2 = p23.enter_context(nc.Block("p23"))

                def OCG(tc, oc):
                    return tc * OCQ + oc

                def QKG(tc, oc):
                    return tc * 2 * J + oc

                def GRG(tc, s, g):
                    return (tc * NSUB + s) * 8 + g

                def CHG(tc, s, g, c):
                    return GRG(tc, s, g) * 4 + c

                def HFG(tc, s, g, hh):
                    return GRG(tc, s, g) * 2 + hh

                def B64(tc, s, half64):
                    return (tc * NSUB + s) * 2 + half64

                wsc = {0: wqgb, 1: wkb, 2: wvb}

                @blk2.gpsimd
                def _(g):
                    _inc(g.memset(bq64[0][:, :], 1.0), s_ini)
                    _inc(g.memset(bq64[1][:, :], 1.0), s_ini)
                    _inc(g.memset(bk64[0][:, :], 1.0), s_ini)
                    _inc(g.memset(bk64[1][:, :], 1.0), s_ini)
                    _inc(g.memset(vall[:, J * TCH:(J + 1) * TCH], 1.0), s_ini)
                    g.wait_ge(s_pw.h, s_pw.exp(3 * EC))
                    g.wait_ge(s_xb.h, s_xb.exp(4))
                    for tc in range(NTCH):
                        if tc >= 2:
                            g.wait_ge(s_m2.h, s_m2.exp((tc - 1) * OCQ))
                        if tc >= 1:
                            g.wait_ge(s_x2.h, s_x2.exp(tc))
                        src_ = bass.AP(xnb.tensor, tc * 512,
                                       [[EC * T, 128], [T, EC], [1, 512]])
                        _inc(g.dma_start(xnt2[tc % 2][:, :], src_), s_x2)
                        for oc in range(OCQ):
                            O = OCG(tc, oc)
                            if O >= 2:
                                g.wait_ge(s_m2.h, s_m2.exp(O - 1))
                            if O >= 1:
                                g.wait_ge(s_wc.h, s_wc.exp(O))
                            j = oc % J
                            src = bass.AP(wsc[oc // J].tensor, j * EMB * 128,
                                          [[128, 128], [128 * 128, EC], [1, 128]])
                            _inc(g.dma_start(
                                wcol[O % 2][:, :].rearrange("p (ec d) -> p ec d", d=128),
                                src), s_wc)

                @blk2.tensor
                def _(t):
                    t.wait_ge(s_ini.h, 7)
                    for tc in range(NTCH):
                        for oc in range(OCQ + 1):
                            if oc < OCQ:
                                O = OCG(tc, oc)
                                t.wait_ge(s_wc.h, s_wc.exp(O + 1))
                                if oc == 0:
                                    t.wait_ge(s_x2.h, s_x2.exp(tc + 1))
                                if O >= 2:
                                    t.wait_ge(s_e2.h, s_e2.exp(O - 1))
                                if oc < 2 and tc >= 1:
                                    t.wait_ge(s_rm.h, s_rm.exp(tc * NSUB * 32))
                                for ec in range(EC):
                                    i_ = t.matmul(pqk[O % 2][:, :],
                                                  wcol[O % 2][:, ec * 128:(ec + 1) * 128],
                                                  xnt2[tc % 2][:, ec * 512:(ec + 1) * 512],
                                                  start=(ec == 0), stop=(ec == EC - 1))
                                    if ec == EC - 1:
                                        _inc(i_, s_m2)
                            ocp = oc - 1
                            if 0 <= ocp < 2 * J:
                                Q = QKG(tc, ocp)
                                t.wait_ge(s_e2.h, s_e2.exp(OCG(tc, ocp) + 1))
                                if Q >= 2:
                                    t.wait_ge(s_te.h, s_te.exp(NSUB * (Q - 1)))
                                if ocp == 0 and tc >= 1:
                                    t.wait_ge(s_ne.h, s_ne.exp(tc * NGT))
                                for sub in range(NSUB):
                                    _inc(t.transpose(ptr2[Q % 2][:, sub * 128:(sub + 1) * 128],
                                                     qgd[Q % 2][:, sub * 128:(sub + 1) * 128],
                                                     ident[:, :]), s_t2)
                        for s in range(NSUB):
                            for g in range(8 + 2):
                                if g < 8:
                                    G = GRG(tc, s, g)
                                    if g % 4 == 0:
                                        HS = (tc * NSUB + s) * 2 + g // 4
                                        t.wait_ge(s_bl.h, s_bl.exp((HS + 1) * 2 * J))
                                    for c in range(4):
                                        C = CHG(tc, s, g, c)
                                        if C >= 2:
                                            t.wait_ge(s_rm.h, s_rm.exp(C - 1))
                                        if s == 0 and g == 0 and c < 2:
                                            t.wait_ge(s_e2.h, s_e2.exp(OCG(tc, OCQ - 1) + 1))
                                        for i in range(4):
                                            tt = g * 16 + c * 4 + i
                                            hs2 = tt // 64
                                            to = tt % 64
                                            mm = t.matmul(
                                                pqk[C % 2][:, i * 128:(i + 1) * 128],
                                                bq64[hs2 % 2][0:J, to * 128:(to + 1) * 128],
                                                bk64[hs2 % 2][0:J, to * 128:(to + 1) * 128],
                                                start=True, stop=True)
                                            if i == 3:
                                                _inc(mm, s_m1)
                                gp = g - 1
                                if 0 <= gp < 8:
                                    G = GRG(tc, s, gp)
                                    t.wait_ge(s_rm.h, s_rm.exp(4 * (G + 1)))
                                    if G >= 1:
                                        t.wait_ge(s_ne.h, s_ne.exp(G))
                                    _inc(t.transpose(ptr2[0][0:16, 0:128], nm16[G % 2][:, :],
                                                     ident[:, :]), s_nt)
                                gq = g - 2
                                if 0 <= gq < 8:
                                    G = GRG(tc, s, gq)
                                    for c in range(4):
                                        C = CHG(tc, s, gq, c)
                                        t.wait_ge(s_ex.h, s_ex.exp(C + 1))
                                        hh = c // 2
                                        HF = HFG(tc, s, gq, hh)
                                        if c % 2 == 0 and HF >= 2:
                                            t.wait_ge(s_no.h, s_no.exp(HF - 1))
                                        for i in range(4):
                                            tt = gq * 16 + c * 4 + i
                                            ttc = s * 128 + tt
                                            k = (c % 2) * 4 + i
                                            rhs = bass.AP(vall, ttc,
                                                          [[(J + 1) * TCH, 128], [TCH, J + 1], [0, 4]])
                                            mm = t.matmul(pav[HF % 2][:, k * VC:(k + 1) * VC],
                                                          expt[C % 4][:, i * 128:(i + 1) * 128],
                                                          rhs, start=True, stop=True)
                                            if i == 3:
                                                _inc(mm, s_m3)
                                if 0 <= gp < 8:
                                    G = GRG(tc, s, gp)
                                    t.wait_ge(s_in.h, s_in.exp(G + 1))
                                    for c in range(4):
                                        C = CHG(tc, s, gp, c)
                                        if C >= 2:
                                            t.wait_ge(s_ex.h, s_ex.exp(C - 1))
                                        for i in range(4):
                                            tt = gp * 16 + c * 4 + i
                                            hs2 = tt // 64
                                            to = tt % 64
                                            mm = t.matmul(
                                                pt3[C % 2][:, i * 128:(i + 1) * 128],
                                                bk64[hs2 % 2][0:J + 1, to * 128:(to + 1) * 128],
                                                bq64[hs2 % 2][0:J + 1, to * 128:(to + 1) * 128],
                                                start=True, stop=True)
                                            if i == 3:
                                                _inc(mm, s_mT)

                @blk2.vector
                def _(v):
                    for tc in range(NTCH):
                        for oc in range(OCQ):
                            O = OCG(tc, oc)
                            j = oc % J
                            v.wait_ge(s_m2.h, s_m2.exp(O + 1))
                            if oc < 2 * J:
                                Q = QKG(tc, oc)
                                if Q >= 2:
                                    v.wait_ge(s_t2.h, s_t2.exp(NSUB * (Q - 1)))
                                bias = bqg if oc < J else bkc
                                _inc(v.tensor_scalar(qgd[Q % 2][:, :], pqk[O % 2][:, :],
                                                     bias[:, j:j + 1], None, ALU.add), s_e2)
                            else:
                                if oc == 2 * J and tc >= 1:
                                    v.wait_ge(s_m3.h, s_m3.exp(tc * NSUB * 32))
                                _inc(v.tensor_scalar(vall[:, j * TCH:(j + 1) * TCH],
                                                     pqk[O % 2][:, :],
                                                     bvc[:, j:j + 1], None, ALU.add), s_e2)
                        for s in range(NSUB):
                            for g in range(8 + 2):
                                if g < 8:
                                    G = GRG(tc, s, g)
                                    for c in range(4):
                                        C = CHG(tc, s, g, c)
                                        v.wait_ge(s_m1.h, s_m1.exp(C + 1))
                                        if c == 0 and G >= 2:
                                            v.wait_ge(s_nt.h, s_nt.exp(G - 1))
                                        _inc(v.tensor_reduce(
                                            nm16[G % 2][:, c * 4:(c + 1) * 4],
                                            pqk[C % 2][:, :].rearrange("p (i e) -> p i e", i=4),
                                            AX.X, ALU.max, negate=True), s_rm)
                                gq = g - 2
                                if 0 <= gq < 8:
                                    G = GRG(tc, s, gq)
                                    for hh in range(2):
                                        HF = HFG(tc, s, gq, hh)
                                        v.wait_ge(s_m3.h, s_m3.exp(4 * G + 2 * hh + 2))
                                        rsum = bass.AP(pav[HF % 2], 4 * J,
                                                       [[512, 128], [VC, 8]])
                                        _inc(v.reciprocal(sray[HF % 2][:, :], rsum), s_rc)
                                        v.wait_ge(s_rc.h, s_rc.exp(HF + 1))
                                        hid = (gq * 2 + hh) % 8
                                        half64 = (gq * 2 + hh) // 8
                                        B = B64(tc, s, half64)
                                        if B >= 2 and hid == 0:
                                            v.wait_ge(s_as.h, s_as.exp(B - 1))
                                        for k in range(8):
                                            tok64 = hid * 8 + k
                                            ts_ = v.tensor_scalar(
                                                avb[B % 2][:, tok64 * H:(tok64 + 1) * H],
                                                pav[HF % 2][:, k * VC:k * VC + H],
                                                sray[HF % 2][:, k:k + 1], None, ALU.mult)
                                            if k == 7:
                                                _inc(ts_, s_no)

                @blk2.scalar
                def _(sc):
                    for tc in range(NTCH):
                        for ocp in range(2 * J):
                            Q = QKG(tc, ocp)
                            if ocp == 0 and tc >= 1:
                                sc.wait_ge(s_bl.h, s_bl.exp(tc * NSUB * 2 * J))
                            for sub in range(NSUB):
                                sc.wait_ge(s_t2.h, s_t2.exp(NSUB * (Q + 1)))
                                _inc(sc.copy(
                                    qgt[:, (ocp * NSUB + sub) * 128:(ocp * NSUB + sub + 1) * 128],
                                    ptr2[Q % 2][:, sub * 128:(sub + 1) * 128]), s_te)
                        for s in range(NSUB):
                            for g in range(8 + 2):
                                gp = g - 1
                                if 0 <= gp < 8:
                                    G = GRG(tc, s, gp)
                                    sc.wait_ge(s_nt.h, s_nt.exp(G + 1))
                                    if G >= 2:
                                        sc.wait_ge(s_in.h, s_in.exp(G - 1))
                                    _inc(sc.copy(nmr[G % 2][:, :], ptr2[0][0:16, 0:128]), s_ne)
                                    for c in range(4):
                                        C = CHG(tc, s, gp, c)
                                        sc.wait_ge(s_mT.h, s_mT.exp(C + 1))
                                        if C >= 4:
                                            sc.wait_ge(s_m3.h, s_m3.exp(C - 3))
                                        _inc(sc.activation(expt[C % 4][:, :], pt3[C % 2][:, :],
                                                           AF.Exp), s_ex)
                            for half64 in range(2):
                                B = B64(tc, s, half64)
                                sc.wait_ge(s_no.h, s_no.exp((B + 1) * 8))
                                if B >= 1:
                                    sc.wait_ge(s_as.h, s_as.exp(B))
                                base = tc * TCH + s * 128 + half64 * 64
                                dst = bass.AP(attn_s.tensor, base * EMB,
                                              [[H, 128], [EMB, 64], [1, H]])
                                _inc(sc.dma_start(dst, avb[B % 2][:, :].rearrange(
                                    "p (t h) -> p t h", h=H)), s_as)

                @blk2.sync
                def _(s):
                    for tc in range(NTCH):
                        for sub in range(NSUB):
                            SB = tc * NSUB + sub
                            for hs in range(2):
                                HS = SB * 2 + hs
                                s.wait_ge(s_te.h, s_te.exp((tc * 2 * J + 2 * J) * NSUB))
                                if HS >= 1:
                                    s.wait_ge(s_bl.h, s_bl.exp(HS * 2 * J))
                                if HS >= 2:
                                    s.wait_ge(s_mT.h, s_mT.exp((HS - 1) * 16))
                                for j in range(J):
                                    src = qgt[hs * 64:hs * 64 + 64,
                                              (j * NSUB + sub) * 128:(j * NSUB + sub + 1) * 128]
                                    dst = bass.AP(bq64[hs % 2], j * 64 * 128,
                                                  [[64 * 128, 1], [128, 64], [1, 128]])
                                    _inc(s.dma_start(dst, src), s_bl)
                                for j in range(J):
                                    src = qgt[hs * 64:hs * 64 + 64,
                                              ((J + j) * NSUB + sub) * 128:((J + j) * NSUB + sub + 1) * 128]
                                    dst = bass.AP(bk64[hs % 2], j * 64 * 128,
                                                  [[64 * 128, 1], [128, 64], [1, 128]])
                                    _inc(s.dma_start(dst, src), s_bl)
                            for g in range(8):
                                G = GRG(tc, sub, g)
                                s.wait_ge(s_ne.h, s_ne.exp(G + 1))
                                if G >= 1:
                                    s.wait_ge(s_in.h, s_in.exp(G))
                                dst = bass.AP(bq64[(g // 4) % 2],
                                              J * 64 * 128 + (g % 4) * 16 * 128,
                                              [[64 * 128, 1], [128, 16], [1, 128]])
                                _inc(s.dma_start(dst, nmr[G % 2][:, :]), s_in)

            nc.all_engine_barrier()

        if phases < 3:
            return nc
        with ExitStack() as p4:
            wdT = p4.enter_context(nc.sbuf_tensor("wdT", [128, NF * EP], BF16))
            ptsb = [p4.enter_context(nc.sbuf_tensor(f"ptsb{i}", [128, NF * 128], BF16)) for i in range(3)]
            xsl = [p4.enter_context(nc.sbuf_tensor(f"xsl{i}", [128, EP], F32)) for i in range(2)]
            asl = [p4.enter_context(nc.sbuf_tensor(f"asl{i}", [128, EP], F32)) for i in range(2)]
            osb = [p4.enter_context(nc.sbuf_tensor(f"osb{i}", [128, EP], F32)) for i in range(2)]
            py = [p4.enter_context(nc.psum_tensor(f"py{i}", [128, EP], F32)) for i in range(2)]

            blk4 = p4.enter_context(nc.Block("p4"))

            @blk4.sync
            def _(s):
                s.wait_ge(s_cd.h, s_cd.exp(EMB // 256))
                WB = 8
                for p in range(NEP):
                    if p >= 1:
                        s.wait_ge(s_my.h, s_my.exp(p * NTG))
                    for fc in range(NF):
                        if fc % WB == 0 and p * NF + fc > 0:
                            s.wait_ge(s_wt.h, s_wt.exp(p * NF + fc))
                        _inc(s.dma_start_transpose(
                            wdT[:, fc * EP:(fc + 1) * EP],
                            wdb[p * EP:(p + 1) * EP, fc * 128:(fc + 1) * 128]), s_wt)
                    for tg in range(NTG):
                        PTG = p * NTG + tg
                        s.wait_ge(s_cb.h, s_cb.exp(2 * (PTG + 1)))
                        if PTG >= 1:
                            s.wait_ge(s_ou.h, s_ou.exp(PTG))
                        _inc(s.dma_start(out[tg * 128:(tg + 1) * 128, p * EP:(p + 1) * EP],
                                         osb[PTG % 2][:, :]), s_ou)
                s.wait_ge(s_ou.h, s_ou.exp(NEP * NTG))

            @blk4.gpsimd
            def _(g):
                g.wait_ge(s_pt.h, s_pt.exp(NF))
                NH = NF // 2
                for p in range(NEP):
                    for tg in range(NTG):
                        PTG = p * NTG + tg
                        if PTG >= 3:
                            g.wait_ge(s_my.h, s_my.exp(PTG - 2))
                        for hf in range(2):
                            src = bass.AP(pts.tensor, hf * NH * T * 128 + tg * 128 * 128,
                                          [[128, 128], [T * 128, NH], [1, 128]])
                            dst = ptsb[PTG % 3][:, hf * NH * 128:(hf + 1) * NH * 128].rearrange(
                                "p (f t) -> p f t", t=128)
                            _inc(g.dma_start(dst, src), s_p4[PTG % 3])

            @blk4.scalar
            def _(sc):
                sc.wait_ge(s_as.h, s_as.exp(T // 64))
                for p in range(NEP):
                    for tg in range(NTG):
                        PTG = p * NTG + tg
                        if PTG >= 2:
                            sc.wait_ge(s_cb.h, s_cb.exp(2 * (PTG - 1)))
                        if PTG >= 1:
                            sc.wait_ge(s_xa.h, s_xa.exp(2 * PTG))
                        _inc(sc.dma_start(xsl[PTG % 2][:, :],
                                          x_in[tg * 128:(tg + 1) * 128, p * EP:(p + 1) * EP]), s_xa)
                        _inc(sc.dma_start(asl[PTG % 2][:, :],
                                          attn_s[tg * 128:(tg + 1) * 128, p * EP:(p + 1) * EP]), s_xa)

            @blk4.tensor
            def _(t):
                for p in range(NEP):
                    for tg in range(NTG):
                        PTG = p * NTG + tg
                        t.wait_ge(s_p4[PTG % 3].h, s_p4[PTG % 3].exp(2 * (PTG // 3 + 1)))
                        if PTG >= 2:
                            t.wait_ge(s_cb.h, s_cb.exp(2 * (PTG - 1)))
                        for fc in range(NF):
                            if tg == 0 and fc % 8 == 0:
                                t.wait_ge(s_wt.h, s_wt.exp(p * NF + min(NF, fc + 8)))
                            i_ = t.matmul(py[PTG % 2][:, :],
                                          ptsb[PTG % 3][:, fc * 128:(fc + 1) * 128],
                                          wdT[:, fc * EP:(fc + 1) * EP],
                                          start=(fc == 0), stop=(fc == NF - 1))
                            if fc == NF - 1:
                                _inc(i_, s_my)

            @blk4.vector
            def _(v):
                for p in range(NEP):
                    for tg in range(NTG):
                        PTG = p * NTG + tg
                        v.wait_ge(s_my.h, s_my.exp(PTG + 1))
                        v.wait_ge(s_xa.h, s_xa.exp(2 * (PTG + 1)))
                        if PTG >= 2:
                            v.wait_ge(s_ou.h, s_ou.exp(PTG - 1))
                        _inc(v.tensor_add(osb[PTG % 2][:, :], xsl[PTG % 2][:, :],
                                          py[PTG % 2][:, :]), s_cb)
                        v.wait_ge(s_cb.h, s_cb.exp(2 * PTG + 1))
                        _inc(v.tensor_add(osb[PTG % 2][:, :], osb[PTG % 2][:, :],
                                          asl[PTG % 2][:, :]), s_cb)

    return nc


_PROG_CACHE = {}


def _get_prog(T, EMB, FF):
    key = (T, EMB, FF)
    if key not in _PROG_CACHE:
        _PROG_CACHE[key] = build_program(T, EMB, FF)
    return _PROG_CACHE[key]


def kernel(L0_attn_x, L0_attn_Wq, L0_attn_bq, L0_attn_Wk, L0_attn_bk,
           L0_attn_Wv, L0_attn_bv, L0_ffn_W_gate, L0_ffn_W_up, L0_ffn_W_down):
    x = np.ascontiguousarray(np.asarray(L0_attn_x, dtype=np.float32))
    B, S, EMB = x.shape
    NTOK = B * S
    T = NTOK // N_CORES
    x2 = x.reshape(NTOK, EMB)
    wq2 = np.ascontiguousarray(np.asarray(L0_attn_Wq, dtype=np.float32).reshape(EMB, -1))
    wk2 = np.ascontiguousarray(np.asarray(L0_attn_Wk, dtype=np.float32).reshape(EMB, -1))
    wv2 = np.ascontiguousarray(np.asarray(L0_attn_Wv, dtype=np.float32).reshape(EMB, -1))
    bq2 = np.ascontiguousarray(np.asarray(L0_attn_bq, dtype=np.float32))
    bk2 = np.ascontiguousarray(np.asarray(L0_attn_bk, dtype=np.float32))
    bv2 = np.ascontiguousarray(np.asarray(L0_attn_bv, dtype=np.float32))
    wg2 = np.ascontiguousarray(np.asarray(L0_ffn_W_gate, dtype=np.float32))
    wu2 = np.ascontiguousarray(np.asarray(L0_ffn_W_up, dtype=np.float32))
    wd2 = np.ascontiguousarray(np.asarray(L0_ffn_W_down, dtype=np.float32))
    FF = wg2.shape[0]

    nc = _get_prog(T, EMB, FF)
    in_maps = []
    for c in range(N_CORES):
        in_maps.append({
            "x": x2[c * T:(c + 1) * T],
            "wq": wq2, "wk": wk2, "wv": wv2,
            "bq": bq2, "bk": bk2, "bv": bv2,
            "wg": wg2, "wu": wu2, "wd": wd2,
        })
    res = run_bass_kernel_spmd(nc, in_maps, core_ids=list(range(N_CORES)))
    outs = [np.asarray(res.results[c]["out"]) for c in range(N_CORES)]
    return np.concatenate(outs, axis=0).reshape(B, S, EMB).astype(np.float32)
